# version 01
# baseline (speedup 1.0000x reference)
import math
import numpy as np
import ml_dtypes
import concourse.bass as bass
import concourse.mybir as mybir
from concourse.bass_utils import run_bass_kernel_spmd

F32 = mybir.dt.float32
BF16 = mybir.dt.bfloat16
AF = mybir.ActivationFunctionType
ALU = mybir.AluOpType
AX = mybir.AxisListType.X

BF = ml_dtypes.bfloat16
NB, SEQ, DM = 4, 4096, 1024
TOW = 2048
NCH = DM // 128
NEG = -1.0e30

AP = bass.AP


def cdiv(a, b):
    return (a + b - 1) // b



def to_fm(x):
    return np.ascontiguousarray(np.transpose(x, (0, 2, 1)))


def from_fm(x_fm):
    return np.ascontiguousarray(np.transpose(x_fm, (0, 2, 1)))


def vec_chunks(v):
    n = v.shape[0] // 128
    return np.ascontiguousarray(v.reshape(n, 128).T.astype(np.float32))



class Sem:
    def __init__(self, stack, nc, name):
        self.h = stack.enter_context(nc.semaphore(name))
        self.v = 0

    def inc(self, n=1):
        self.v += n
        return self.h, n


class Prog:
    ENGINES = ("sync", "tensor", "vector", "scalar", "gpsimd")

    def __init__(self, nc):
        self.nc = nc
        self.ops = {e: [] for e in self.ENGINES}

    def emit(self, eng, fn, waits=(), inc=None):
        self.ops[eng].append((tuple(waits), fn, inc))

    def dma(self, eng, sem, out, in_, waits=()):
        def fn(e, out=out, in_=in_):
            return e.dma_start(out=out, in_=in_)
        self.emit(eng, fn, waits=tuple(waits) + ((sem.h, sem.v),),
                  inc=sem.inc(16))

    def flush(self, name=None):
        nc = self.nc
        with nc.Block(name) as blk:
            for e in self.ENGINES:
                lst = self.ops[e]
                if not lst:
                    continue
                def runner(engine, lst=lst):
                    for waits, fn, inc in lst:
                        for (h, v) in waits:
                            engine.wait_ge(h, v)
                        inst = fn(engine)
                        if inc is not None:
                            if inst is None:
                                raise RuntimeError("inc on op returning None")
                            inst.then_inc(inc[0], inc[1])
                getattr(blk, e)(runner)
        self.ops = {e: [] for e in self.ENGINES}
        nc.all_engine_barrier()


def load_weight_cols(p, waits, dsem, wdram, d_in, c0, cw, wst_f32, wbf, cast_sem,
                     cast_eng="gpsimd", extra_cast_waits=()):
    nk = d_in // 128
    row = wdram.tensor.shape[-1]
    src = AP(wdram.tensor, wdram.offset + c0, [[row, 128], [128 * row, nk], [1, cw]])
    p.dma("sync", dsem, wst_f32[:, 0:nk * cw], src, waits=waits)
    def cast(e, dst=wbf[:, 0:nk * cw], s=wst_f32[:, 0:nk * cw]):
        return e.tensor_copy(dst, s)
    p.emit(cast_eng, cast, waits=[(dsem.h, dsem.v)] + list(extra_cast_waits),
           inc=cast_sem.inc())



def build_causal_mha(t0, kind="l0", stage=9):
    nc = bass.Bass("TRN2", target_bir_lowering=False)
    import contextlib
    st = contextlib.ExitStack()

    xf = nc.dram_tensor("xf", [DM, SEQ], F32, kind="ExternalInput").ap()
    if kind == "l0":
        wat = nc.dram_tensor("wat", [DM, 3 * DM], F32, kind="ExternalInput").ap()
        bat = nc.dram_tensor("bat", [128, 24], F32, kind="ExternalInput").ap()
    else:
        wq = nc.dram_tensor("wq", [DM, DM], F32, kind="ExternalInput").ap()
        wk = nc.dram_tensor("wk", [DM, DM], F32, kind="ExternalInput").ap()
        wv = nc.dram_tensor("wv", [DM, DM], F32, kind="ExternalInput").ap()
        bqkv = nc.dram_tensor("bqkv", [128, 24], F32, kind="ExternalInput").ap()
        jrow = nc.dram_tensor("jrow", [128, SEQ], F32, kind="ExternalInput").ap()
    wpr = nc.dram_tensor("wpr", [DM, DM], F32, kind="ExternalInput").ap()
    bpr = nc.dram_tensor("bpr", [128, 8], F32, kind="ExternalInput").ap()
    masks = nc.dram_tensor("masks", [128, 4 * 512], F32, kind="ExternalInput").ap()
    ident_in = nc.dram_tensor("ident_in", [128, 128], BF16, kind="ExternalInput").ap()
    xo = nc.dram_tensor("xo", [DM, TOW], F32, kind="ExternalOutput").ap()

    t = lambda *a: st.enter_context(nc.sbuf_tensor(*a))
    xr = [t(f"xr{c}", [128, TOW], F32) for c in range(NCH)]
    xbf = [xr[c][:, :].bitcast(BF16) for c in range(NCH)]
    q_bf = t("q_bf", [128, TOW], BF16)
    k_bf = t("k_bf", [128, SEQ], BF16)
    v_fm = t("v_fm", [128, SEQ], BF16)
    vtok = t("vtok", [128, SEQ], BF16)
    o_bf = [t(f"o{c}", [128, TOW], BF16) for c in range(NCH)]
    s_row = t("s_row", [128, SEQ], F32)
    p_bf = t("p_bf", [128, SEQ], BF16)
    mask_sb = t("mask_sb", [128, 4 * 512], F32)
    ident = t("ident", [128, 128], BF16)
    w32 = [t(f"w32_{i}", [128, 1024], F32) for i in range(2)]
    wbf = [t(f"wbf_{i}", [128, 1024], BF16) for i in range(2)]
    bias_sb = t("bias_sb", [128, 24], F32)
    bpr_sb = t("bpr_sb", [128, 8], F32)
    pt_sb = [t(f"pt{i}", [128, 128], BF16) for i in range(2)]
    mp = t("mp", [128, 8], F32)
    negm = t("negm", [128, 1], F32)
    den = t("den", [128, 1], F32)
    rden = t("rden", [128, 1], F32)
    sq_bf = t("sq_bf", [128, 512], BF16)
    if kind == "l2":
        j_sb = t("j_sb", [128, SEQ], F32)
    ones_bf = t("ones_bf", [128, 1], BF16)
    ones_row = t("ones_row", [1, 128], F32)
    stat = t("stat", [1, 3 * 512], F32)
    stat2 = t("stat2", [1, 2 * 512], F32)
    out_f = t("out_f", [128, 512], F32)

    ps = st.enter_context(nc.psum_tensor("ps", [128, 2 * 512], F32))
    mm_ps = [ps.ap()[:, 0:512], ps.ap()[:, 512:1024]]
    pt_ps_t = st.enter_context(nc.psum_tensor("pt_ps", [128, 2 * 1024], BF16))
    pt_ps = [pt_ps_t.ap()[:, 0:128], pt_ps_t.ap()[:, 1024:1152]]
    out_ps = st.enter_context(nc.psum_tensor("out_ps", [64, 128], F32))
    st_ps = st.enter_context(nc.psum_tensor("st_ps", [1, 512], F32))
    st2_ps = st.enter_context(nc.psum_tensor("st2_ps", [1, 512], F32))
    bc_ps = mm_ps[0]
    bc2_ps = mm_ps[1]

    ld = Sem(st, nc, "ld")
    sg = Sem(st, nc, "sg")
    sp = Sem(st, nc, "sp")
    sv = Sem(st, nc, "sv")
    sa = Sem(st, nc, "sa")
    so = Sem(st, nc, "so")

    p = Prog(nc)

    p.dma("sync", ld, mask_sb[:, :], masks)
    if kind == "l2":
        p.dma("sync", ld, j_sb[:, :], jrow)
    p.dma("sync", ld, ident[:, :], ident_in)
    p.dma("sync", ld, bias_sb[:, :], bat if kind == "l0" else bqkv)
    p.dma("sync", ld, bpr_sb[:, :], bpr)
    for c in range(NCH):
        for hh in range(2):
            dst = s_row[:, hh * 2048:(hh + 1) * 2048]
            src = xf[c * 128:(c + 1) * 128, hh * 2048:(hh + 1) * 2048]
            waits = [(sv.h, sv.v)] if (c + hh) else []
            p.dma("sync", ld, dst, src, waits=waits)
            def cast(e, d=xbf[c][:, hh * 2048:(hh + 1) * 2048], s=dst):
                return e.tensor_copy(d, s)
            p.emit("vector", cast, waits=[(ld.h, ld.v)], inc=sv.inc())
    p.flush("load_x")

    def _early_out():
        p.dma("sync", ld, xo[0:128, 0:512], s_row[:, 0:512])
        def fin(e):
            e.wait_ge(ld.h, ld.v)
            return None
        p.emit("sync", fin)
        p.flush("early")
        st.close()
        return nc
    if stage <= 1:
        return _early_out()

    NKT = 8
    for c in range(NCH):
        if kind == "l0":
            specs = [("q", wat, c * 128, c), ("k", wat, DM + c * 128, 8 + c),
                     ("v", wat, 2 * DM + c * 128, 16 + c)]
        else:
            specs = [("q", wq, c * 128, c), ("k", wk, c * 128, 8 + c),
                     ("v", wv, c * 128, 16 + c)]
        for i, (role, wsrc, c0, bidx) in enumerate(specs):
            buf = i % 2
            load_weight_cols(p, [(sp.h, sp.v), (so.h, so.v)], ld, wsrc, DM, c0, 128,
                             w32[buf], wbf[buf], so,
                             extra_cast_waits=[(sp.h, sp.v)])
            ntt = 4 if role == "q" else 8
            toff = t0 if role == "q" else 0
            for tt in range(ntt):
                slot = tt % 2
                tsl = slice(toff + tt * 512, toff + (tt + 1) * 512)
                lsl = slice(tt * 512, (tt + 1) * 512)
                for k in range(NCH):
                    def mm(e, pa=mm_ps[slot], w=wbf[buf][:, k * 128:(k + 1) * 128],
                           r=xbf[k][:, tsl], st_=(k == 0), sp_=(k == NCH - 1)):
                        return e.matmul(pa, w, r, start=st_, stop=sp_)
                    waits = []
                    if k == 0:
                        waits = [(so.h, so.v), (sv.h, sv.v)]
                    p.emit("tensor", mm, waits=waits, inc=sp.inc() if k == NCH - 1 else None)
                bcol = bias_sb[:, bidx:bidx + 1]
                if role == "q":
                    def ev(e, d=q_bf[:, lsl], s=mm_ps[slot], b=bcol):
                        return e.tensor_scalar(d, s, b, 0.125, ALU.add, ALU.mult)
                elif role == "k":
                    def ev(e, d=k_bf[:, lsl], s=mm_ps[slot], b=bcol):
                        return e.tensor_scalar(d, s, b, None, ALU.add)
                else:
                    def ev(e, d=v_fm[:, lsl], s=mm_ps[slot], b=bcol):
                        return e.tensor_scalar(d, s, b, None, ALU.add)
                p.emit("vector", ev, waits=[(sp.h, sp.v)], inc=sv.inc())

        trev = {}
        for kc in range(SEQ // 128):
            slot = kc % 2
            def tr(e, pa=pt_ps[slot], s=v_fm[:, kc * 128:(kc + 1) * 128], idt=ident[:, :]):
                return e.transpose(pa, s, idt)
            waits = [(sv.h, sv.v)] if kc < 2 else [(sv.h, trev[kc - 2])]
            p.emit("tensor", tr, waits=waits, inc=sp.inc())
            def tev(e, d=vtok[:, kc * 128:(kc + 1) * 128], s=pt_ps[slot]):
                return e.tensor_copy(d, s)
            p.emit("vector", tev, waits=[(sp.h, sp.v)], inc=sv.inc())
            trev[kc] = sv.v
        p.flush(f"kv{c}")
        if stage <= 2 and c == 0:
            return _early_out()

        for hh in range(2):
            h = 2 * c + hh
            hb = 64 * hh
            q_h = q_bf[hb:hb + 64, :]
            k_h = k_bf[hb:hb + 64, :]
            last_exp = sa.v
            last_tp = sp.v
            for qt in range(16):
                q0 = t0 + qt * 128
                nlive = q0 // 512 + 1
                evac_done = {}
                for kt in range(nlive):
                    slot = kt % 2
                    def mm(e, pa=mm_ps[slot], l=q_h[:, qt * 128:(qt + 1) * 128],
                           r=k_h[:, kt * 512:(kt + 1) * 512]):
                        return e.matmul(pa, l, r, start=True, stop=True)
                    waits = [(sv.h, evac_done[kt - 2])] if kt >= 2 else [(sv.h, sv.v)]
                    p.emit("tensor", mm, waits=waits, inc=sp.inc())
                    diag = (kt == q0 // 512)
                    ssl = s_row[:, kt * 512:(kt + 1) * 512]
                    if kind == "l2":
                        slope = 2.0 ** (-0.5 * (h + 1))
                        def ev0(e, d=ssl, s=mm_ps[slot],
                                j=j_sb[:, kt * 512:(kt + 1) * 512], sl=slope):
                            return e.scalar_tensor_tensor(d, j, sl, s, ALU.mult, ALU.add)
                        p.emit("vector", ev0,
                               waits=[(sp.h, sp.v), (sa.h, last_exp)], inc=sv.inc())
                        if diag:
                            mv = (q0 - 512 * kt) // 128
                            def evm(e, d=ssl, a=ssl,
                                    m=mask_sb[:, mv * 512:(mv + 1) * 512]):
                                return e.tensor_add(d, a, m)
                            p.emit("vector", evm, waits=[(sv.h, sv.v)], inc=sv.inc())
                        def ev2(e, d=mp[:, kt:kt + 1], s=ssl):
                            return e.tensor_reduce(d, s, AX, ALU.max)
                        p.emit("vector", ev2, waits=[(sv.h, sv.v)], inc=sv.inc())
                    elif diag:
                        mv = (q0 - 512 * kt) // 128
                        def ev1(e, d=ssl, s=mm_ps[slot],
                                m=mask_sb[:, mv * 512:(mv + 1) * 512]):
                            return e.scalar_tensor_tensor(d, s, 1.0, m, ALU.mult, ALU.add)
                        p.emit("vector", ev1,
                               waits=[(sp.h, sp.v), (sa.h, last_exp)], inc=sv.inc())
                        def ev2(e, d=mp[:, kt:kt + 1], s=ssl):
                            return e.tensor_reduce(d, s, AX, ALU.max)
                        p.emit("vector", ev2, waits=[(sv.h, sv.v)], inc=sv.inc())
                    else:
                        def ev1(e, d=ssl, s=mm_ps[slot], a=mp[:, kt:kt + 1]):
                            return e.tensor_scalar(d, s, 0.0, None, ALU.add,
                                                   ALU.max, accum_out=a)
                        p.emit("vector", ev1,
                               waits=[(sp.h, sp.v), (sa.h, last_exp)], inc=sv.inc())
                    evac_done[kt] = sv.v
                def ng(e, d=negm[:, :], s=mp[:, 0:nlive]):
                    return e.tensor_reduce(d, s, AX, ALU.max, negate=True)
                p.emit("vector", ng, waits=[(sv.h, sv.v)], inc=sv.inc())
                L = nlive * 512
                def ex(e, d=p_bf[:, 0:L], s=s_row[:, 0:L], b=negm[:, :], a=den[:, :]):
                    return e.activation(d, s, AF.Exp, bias=b, scale=1.0, accum_out=a)
                p.emit("scalar", ex, waits=[(sv.h, sv.v), (sp.h, last_tp)],
                       inc=sa.inc())
                last_exp = sa.v
                def rc(e, d=rden[:, :], s=den[:, :]):
                    return e.reciprocal(d, s)
                p.emit("vector", rc, waits=[(sa.h, sa.v)], inc=sv.inc())
                def nm(e, d=p_bf[:, 0:L], s=p_bf[:, 0:L], r=rden[:, :]):
                    return e.tensor_scalar(d, s, r, None, ALU.mult)
                p.emit("vector", nm, waits=[(sv.h, sv.v)], inc=sv.inc())
                norm_done = sv.v
                nkc = nlive * 4
                ptev = {}
                for kc in range(nkc):
                    slot = kc % 2
                    def tr(e, pa=pt_ps[slot], s=p_bf[:, kc * 128:(kc + 1) * 128],
                           idt=ident[:, :]):
                        return e.transpose(pa, s, idt)
                    w = [(sv.h, norm_done)]
                    if kc >= 2:
                        w.append((sv.h, ptev[kc - 2]))
                    p.emit("tensor", tr, waits=w, inc=sp.inc())
                    def tev(e, d=pt_sb[slot][:, :], s=pt_ps[slot]):
                        return e.tensor_copy(d, s)
                    p.emit("vector", tev, waits=[(sp.h, sp.v)], inc=sv.inc())
                    ptev[kc] = sv.v
                    def vm(e, pa=out_ps.ap()[:, :],
                           l=vtok[:, kc * 128 + hb:kc * 128 + hb + 64],
                           r=pt_sb[slot][:, :], st_=(kc == 0), sp_=(kc == nkc - 1)):
                        return e.matmul(pa, l, r, start=st_, stop=sp_)
                    w = [(sv.h, ptev[kc])]
                    if kc == 0:
                        w.append((sa.h, sa.v))
                    p.emit("tensor", vm, waits=w, inc=sp.inc())
                last_tp = sp.v
                def oev(e, d=o_bf[c][hb:hb + 64, qt * 128:(qt + 1) * 128],
                        s=out_ps.ap()[:, :]):
                    return e.copy(d, s)
                p.emit("scalar", oev, waits=[(sp.h, sp.v)], inc=sa.inc())
            p.flush(f"attn{c}_{hh}")
            if stage <= 3 and c == 0 and hh == 0:
                return _early_out()

    if stage <= 4:
        return _early_out()
    for c in range(NCH):
        p.dma("sync", ld, xr[c][:, :], xf[c * 128:(c + 1) * 128, t0:t0 + TOW])
    p.flush("reload_x")

    EPS = 1e-5
    for oc in range(NCH):
        buf = oc % 2
        load_weight_cols(p, [(sp.h, sp.v), (so.h, so.v)], ld, wpr, DM, oc * 128, 128,
                         w32[buf], wbf[buf], so,
                         extra_cast_waits=[(sp.h, sp.v)])
        for tt in range(4):
            slot = tt % 2
            tsl = slice(tt * 512, (tt + 1) * 512)
            for k in range(NCH):
                def mm(e, pa=mm_ps[slot], w=wbf[buf][:, k * 128:(k + 1) * 128],
                       r=o_bf[k][:, tsl], st_=(k == 0), sp_=(k == NCH - 1)):
                    return e.matmul(pa, w, r, start=st_, stop=sp_)
                waits = [(so.h, so.v), (sv.h, sv.v)] if k == 0 else []
                p.emit("tensor", mm, waits=waits, inc=sp.inc() if k == NCH - 1 else None)
            def ev(e, d=xr[oc][:, tsl], s=mm_ps[slot],
                   b=bpr_sb[:, oc:oc + 1], x=xr[oc][:, tsl]):
                return e.scalar_tensor_tensor(d, s, b, x, ALU.add, ALU.add)
            p.emit("vector", ev, waits=[(sp.h, sp.v)], inc=sv.inc())
    p.flush("oproj")
    if stage <= 5:
        return _early_out()

    def ones_set(e, d=ones_bf[:, :]):
        return e.memset(d, 1.0)
    p.emit("vector", ones_set, inc=sv.inc())
    def ones_set2(e, d=ones_row[0:1, :]):
        return e.memset(d, 1.0)
    p.emit("vector", ones_set2, inc=sv.inc())
    emit_ln(p, nc, xr, None, None, TOW, p_bf, sq_bf, (ones_bf, ones_row), stat, stat2,
            st_ps, st2_ps, bc_ps, bc2_ps, out_f, xo, ld, sp, sv, sa, so, EPS)
    p.flush("ln")

    def fin(e):
        e.wait_ge(ld.h, ld.v)
        return None
    p.emit("sync", fin)
    p.flush("fin")

    st.close()
    return nc


def emit_ln(p, nc, xr, gcol, bcol, ntok, x_bf16_scratch, sq_bf, ones_pair, stat,
            stat2, st_ps, st2_ps, bc_ps, bc2_ps, out_f, xo, ld, sp, sv, sa, so,
            eps, out_bf=None, dma_out=True, in_slices=None):
    ones_bf, ones_row = ones_pair
    if in_slices is None:
        in_slices = lambda c, tsl: xr[c][:, tsl]
    """LayerNorm over feature dim (partitions) of xr (8 fp32 tiles [128, ntok]).

    Writes fp32 result to DRAM xo (and optionally bf16 to out_bf tiles).
    gcol/bcol: optional per-partition gain/bias APs per chunk (list of APs).
    Stats computed from bf16 casts (x and x^2) via ones-matmuls.
    """
    ntt = ntok // 512
    for tt in range(ntt):
        tsl = slice(tt * 512, (tt + 1) * 512)
        for c in range(NCH):
            xb = x_bf16_scratch[:, 0:512]
            def cb(e, d=xb, s=in_slices(c, tsl)):
                return e.tensor_copy(d, s)
            p.emit("gpsimd", cb, waits=[(sp.h, sp.v), (sv.h, sv.v)], inc=so.inc())
            def sqf(e, d=sq_bf[:, :], s=xb):
                return e.activation(d, s, AF.Square)
            p.emit("scalar", sqf, waits=[(so.h, so.v), (sp.h, sp.v)], inc=sa.inc())
            def m1(e, pa=st_ps.ap()[:, :], l=ones_bf[:, :], r=xb,
                   st_=(c == 0), sp_=(c == NCH - 1)):
                return e.matmul(pa, l, r, start=st_, stop=sp_)
            w1_ = [(so.h, so.v)] + ([(sv.h, sv.v)] if c == 0 else [])
            p.emit("tensor", m1, waits=w1_, inc=sp.inc())
            def m2(e, pa=st2_ps.ap()[:, :], l=ones_bf[:, :], r=sq_bf[:, :],
                   st_=(c == 0), sp_=(c == NCH - 1)):
                return e.matmul(pa, l, r, start=st_, stop=sp_)
            p.emit("tensor", m2, waits=[(sa.h, sa.v)], inc=sp.inc())
        ssl = slice(tt * 512, (tt + 1) * 512)
        s1 = stat[0:1, 0:512]
        s2 = stat[0:1, 512:1024]
        rstd = stat[0:1, 1024:1536]
        mrstd = stat2[0:1, 0:512]
        tmp = stat2[0:1, 512:1024]
        def c1(e, d=s1, s=st_ps.ap()[:, :]):
            return e.tensor_scalar(d, s, 1.0 / DM, None, ALU.mult)
        p.emit("vector", c1, waits=[(sp.h, sp.v)], inc=sv.inc())
        def c2(e, d=s2, s=st2_ps.ap()[:, :]):
            return e.tensor_scalar(d, s, 1.0 / DM, None, ALU.mult)
        p.emit("vector", c2, waits=[(sv.h, sv.v)], inc=sv.inc())
        def c3(e, d=tmp, a=s1, b=s1):
            return e.tensor_tensor(d, a, b, ALU.mult)
        p.emit("vector", c3, waits=[(sv.h, sv.v)], inc=sv.inc())
        def c4(e, d=tmp, a=s2, b=tmp):
            return e.scalar_tensor_tensor(d, a, float(eps), b, ALU.add, ALU.subtract)
        p.emit("vector", c4, waits=[(sv.h, sv.v)], inc=sv.inc())
        def c5(e, d=rstd, s=tmp):
            return e.activation(d, s, AF.Ln)
        p.emit("scalar", c5, waits=[(sv.h, sv.v)], inc=sa.inc())
        def c6(e, d=rstd, s=rstd):
            return e.activation(d, s, AF.Exp, scale=-0.5)
        p.emit("scalar", c6, waits=[(sa.h, sa.v)], inc=sa.inc())
        def c7(e, d=mrstd, a=s1, b=rstd):
            return e.tensor_tensor(d, a, b, ALU.mult)
        p.emit("vector", c7, waits=[(sa.h, sa.v)], inc=sv.inc())
        def c8(e, d=mrstd, s=mrstd):
            return e.tensor_scalar(d, s, -1.0, None, ALU.mult)
        p.emit("vector", c8, waits=[(sv.h, sv.v)], inc=sv.inc())
        stats_done = sv.v
        def b1(e, pa=bc_ps, l=ones_row[0:1, :], r=rstd):
            return e.matmul(pa, l, r, start=True, stop=True)
        p.emit("tensor", b1, waits=[(sv.h, stats_done), (sv.h, sv.v)], inc=sp.inc())
        def b2(e, pa=bc2_ps, l=ones_row[0:1, :], r=mrstd):
            return e.matmul(pa, l, r, start=True, stop=True)
        p.emit("tensor", b2, inc=sp.inc())
        bc_done = sp.v
        for c in range(NCH):
            def a1(e, d=out_f[:, :], a=in_slices(c, tsl), b=bc_ps):
                return e.tensor_tensor(d, a, b, ALU.mult)
            a1w = [(sp.h, bc_done), (ld.h, ld.v)]
            if out_bf is not None:
                a1w.append((so.h, so.v))
            p.emit("vector", a1, waits=a1w, inc=sv.inc())
            def a2(e, d=out_f[:, :], a=out_f[:, :], b=bc2_ps):
                return e.tensor_tensor(d, a, b, ALU.add)
            p.emit("vector", a2, waits=[(sv.h, sv.v)], inc=sv.inc())
            if gcol is not None:
                def a3(e, d=out_f[:, :], s=out_f[:, :], g=gcol[c], b=bcol[c]):
                    return e.tensor_scalar(d, s, g, b, ALU.mult, ALU.add)
                p.emit("vector", a3, waits=[(sv.h, sv.v)], inc=sv.inc())
            if out_bf is not None:
                def a4(e, d=out_bf[c][:, tsl], s=out_f[:, :]):
                    return e.tensor_copy(d, s)
                p.emit("gpsimd", a4, waits=[(sv.h, sv.v)], inc=so.inc())
            if dma_out:
                p.dma("sync", ld, xo[c * 128:(c + 1) * 128, tsl], out_f[:, :],
                      waits=[(sv.h, sv.v)])




def build_l2_ffn():
    nc = bass.Bass("TRN2", target_bir_lowering=False)
    import contextlib
    st = contextlib.ExitStack()

    xf = nc.dram_tensor("xf", [DM, TOW], F32, kind="ExternalInput").ap()
    w1 = nc.dram_tensor("w1", [DM, 4096], F32, kind="ExternalInput").ap()
    w2 = nc.dram_tensor("w2", [DM, 4096], F32, kind="ExternalInput").ap()
    h1 = nc.dram_tensor("h1", [4096, 4096], F32, kind="ExternalInput").ap()
    h2 = nc.dram_tensor("h2", [4096, 4096], F32, kind="ExternalInput").ap()
    w3 = nc.dram_tensor("w3", [4096, DM], F32, kind="ExternalInput").ap()
    scl = nc.dram_tensor("scl", [128, 5 * 32], F32, kind="ExternalInput").ap()
    xo = nc.dram_tensor("xo", [DM, TOW], F32, kind="ExternalOutput").ap()

    t = lambda *a: st.enter_context(nc.sbuf_tensor(*a))
    xr = [t(f"xr{c}", [128, TOW], F32) for c in range(NCH)]
    xb = [t(f"xb{c}", [128, TOW], BF16) for c in range(NCH)]
    ga = [t(f"ga{c}", [128, 512], BF16) for c in range(32)]
    gb = [t(f"gb{c}", [128, 512], BF16) for c in range(32)]
    w32 = t("w32", [128, 4096], F32)
    wbf = t("wbf", [128, 4096], BF16)
    scl_sb = t("scl_sb", [128, 5 * 32], F32)
    tmp1 = t("tmp1", [128, 512], F32)
    tmp2 = t("tmp2", [128, 512], F32)
    lnx = t("lnx", [128, 512], BF16)
    sq_bf = t("sq_bf", [128, 512], BF16)
    ones_bf = t("ones_bf", [128, 1], BF16)
    ones_row = t("ones_row", [1, 128], F32)
    stat = t("stat", [1, 3 * 512], F32)
    stat2 = t("stat2", [1, 2 * 512], F32)
    out_f = t("out_f", [128, 512], F32)

    ps = st.enter_context(nc.psum_tensor("ps", [128, 2 * 512], F32))
    mm_ps = [ps.ap()[:, 0:512], ps.ap()[:, 512:1024]]
    ps2 = st.enter_context(nc.psum_tensor("ps2", [128, 2 * 512], F32))
    mm2_ps = [ps2.ap()[:, 0:512], ps2.ap()[:, 512:1024]]
    st_ps = st.enter_context(nc.psum_tensor("st_ps", [1, 512], F32))
    st2_ps = st.enter_context(nc.psum_tensor("st2_ps", [1, 512], F32))
    bc_ps = mm_ps[0]
    bc2_ps = mm_ps[1]

    ld = Sem(st, nc, "ld")
    sp = Sem(st, nc, "sp")
    sv = Sem(st, nc, "sv")
    sa = Sem(st, nc, "sa")
    so = Sem(st, nc, "so")

    p = Prog(nc)

    p.dma("sync", ld, scl_sb[:, :], scl)
    for c in range(NCH):
        p.dma("sync", ld, xr[c][:, :], xf[c * 128:(c + 1) * 128, :])
        def cast(e, d=xb[c][:, :], s=xr[c][:, :]):
            return e.tensor_copy(d, s)
        p.emit("gpsimd", cast, waits=[(ld.h, ld.v)], inc=so.inc())
    p.flush("load_x")

    def matmul_block(dst_ps, wtile, rhs_tiles, tsl, nk, waits0):
        for k in range(nk):
            def mm(e, pa=dst_ps, w=wtile[:, k * 128:(k + 1) * 128],
                   r=rhs_tiles[k][:, tsl], st_=(k == 0), sp_=(k == nk - 1)):
                return e.matmul(pa, w, r, start=st_, stop=sp_)
            p.emit("tensor", mm, waits=(waits0 if k == 0 else []),
                   inc=sp.inc() if k == nk - 1 else None)

    NTT = 4
    TT = 512
    for tt in range(NTT):
        for oc in range(32):
            for role in range(2):
                wsrc = w1 if role == 0 else w2
                load_weight_cols(p, [(sp.h, sp.v), (so.h, so.v)], ld, wsrc, DM,
                                 oc * 128, 128, w32, wbf, so,
                                 extra_cast_waits=[(sp.h, sp.v)])
                for sub in range(TT // 512):
                    tsl = slice(tt * TT + sub * 512, tt * TT + (sub + 1) * 512)
                    dst = (mm_ps if role == 0 else mm2_ps)[sub]
                    matmul_block(dst, wbf, xb, tsl, NCH,
                                 [(so.h, so.v), (sv.h, sv.v), (sa.h, sa.v)])
            for sub in range(TT // 512):
                lsl = slice(sub * 512, (sub + 1) * 512)
                def e1(e, d=tmp1[:, :], s=mm_ps[sub], sc=scl_sb[:, oc:oc + 1]):
                    return e.activation(d, s, AF.Silu, scale=sc)
                p.emit("scalar", e1, waits=[(sp.h, sp.v), (sv.h, sv.v)], inc=sa.inc())
                def e2(e, d=tmp2[:, :], s=mm2_ps[sub], sc=scl_sb[:, 32 + oc:33 + oc]):
                    return e.tensor_scalar(d, s, sc, None, ALU.mult)
                p.emit("vector", e2, waits=[(sp.h, sp.v)], inc=sv.inc())
                def e3(e, d=ga[oc][:, lsl], a=tmp1[:, :], b=tmp2[:, :]):
                    return e.tensor_mul(d, a, b)
                p.emit("vector", e3, waits=[(sa.h, sa.v), (sv.h, sv.v)], inc=sv.inc())
        p.flush(f"s1_{tt}")

        for stage, (src_t, dst_t, wsrc, soff) in enumerate(
                [(ga, gb, h1, 64), (gb, ga, h2, 96)]):
            for oc in range(32):
                load_weight_cols(p, [(sp.h, sp.v), (so.h, so.v)], ld, wsrc, 4096,
                                 oc * 128, 128, w32, wbf, so,
                                 extra_cast_waits=[(sp.h, sp.v)])
                for sub in range(TT // 512):
                    lsl = slice(sub * 512, (sub + 1) * 512)
                    matmul_block(mm_ps[sub], wbf, src_t, lsl, 32,
                                 [(so.h, so.v), (sv.h, sv.v), (sa.h, sa.v)])
                    def e1(e, d=dst_t[oc][:, lsl], s=mm_ps[sub],
                           sc=scl_sb[:, soff + oc:soff + oc + 1]):
                        return e.activation(d, s, AF.Gelu, scale=sc)
                    p.emit("scalar", e1, waits=[(sp.h, sp.v)], inc=sa.inc())
            p.flush(f"s{2 + stage}_{tt}")

        for oc in range(NCH):
            load_weight_cols(p, [(sp.h, sp.v), (so.h, so.v)], ld, w3, 4096,
                             oc * 128, 128, w32, wbf, so,
                             extra_cast_waits=[(sp.h, sp.v)])
            for sub in range(TT // 512):
                lsl = slice(sub * 512, (sub + 1) * 512)
                tsl = slice(tt * TT + sub * 512, tt * TT + (sub + 1) * 512)
                matmul_block(mm_ps[sub], wbf, ga, lsl, 32,
                             [(so.h, so.v), (sv.h, sv.v), (sa.h, sa.v)])
                def e1(e, d=xr[oc][:, tsl], s=mm_ps[sub],
                       sc=scl_sb[:, 128 + oc:129 + oc], x=xr[oc][:, tsl]):
                    return e.scalar_tensor_tensor(d, s, sc, x, ALU.mult, ALU.add)
                p.emit("vector", e1, waits=[(sp.h, sp.v)], inc=sv.inc())
        p.flush(f"w3_{tt}")

    def ones_set(e, d=ones_bf[:, :]):
        return e.memset(d, 1.0)
    p.emit("vector", ones_set, inc=sv.inc())
    def ones_set2(e, d=ones_row[0:1, :]):
        return e.memset(d, 1.0)
    p.emit("vector", ones_set2, inc=sv.inc())
    emit_ln(p, nc, xr, None, None, TOW, lnx, sq_bf, (ones_bf, ones_row), stat,
            stat2, st_ps, st2_ps, bc_ps, bc2_ps, out_f, xo, ld, sp, sv, sa, so,
            1e-5)
    p.flush("ln")
    def fin(e):
        e.wait_ge(ld.h, ld.v)
        return None
    p.emit("sync", fin)
    p.flush("fin")
    st.close()
    return nc


def run_l2_ffn(x_fm, ins):
    scl = np.concatenate([
        vec_chunks(ins["L2_ffn_w1_s"]), vec_chunks(ins["L2_ffn_w2_s"]),
        vec_chunks(ins["L2_ffn_h1_s"]), vec_chunks(ins["L2_ffn_h2_s"]),
        np.concatenate([vec_chunks(ins["L2_ffn_w3_s"]),
                        np.zeros((128, 24), np.float32)], axis=1)], axis=1)
    extras = {
        "w1": np.ascontiguousarray(ins["L2_ffn_w1_w"].T).astype(np.float32),
        "w2": np.ascontiguousarray(ins["L2_ffn_w2_w"].T).astype(np.float32),
        "h1": np.ascontiguousarray(ins["L2_ffn_h1_w"].T).astype(np.float32),
        "h2": np.ascontiguousarray(ins["L2_ffn_h2_w"].T).astype(np.float32),
        "w3": np.ascontiguousarray(ins["L2_ffn_w3_w"].T).astype(np.float32),
        "scl": scl,
    }
    nc = build_l2_ffn()
    in_maps = []
    for core in range(8):
        b, half = core // 2, core % 2
        im = {"xf": np.ascontiguousarray(x_fm[b, :, half * TOW:(half + 1) * TOW])}
        im.update(extras)
        in_maps.append(im)
    res = run_bass_kernel_spmd(nc, in_maps, core_ids=list(range(8)))
    out = np.empty((NB, DM, SEQ), np.float32)
    for core in range(8):
        b, half = core // 2, core % 2
        out[b, :, half * TOW:(half + 1) * TOW] = res.results[core]["xo"]
    return out



def build_l1_ffn():
    nc = bass.Bass("TRN2", target_bir_lowering=False)
    import contextlib
    st = contextlib.ExitStack()

    E, HID = 8, 2048
    xf = nc.dram_tensor("xf", [DM, TOW], F32, kind="ExternalInput").ap()
    wg = nc.dram_tensor("wg", [DM, E], F32, kind="ExternalInput").ap()
    w1 = nc.dram_tensor("w1", [E, DM, HID], F32, kind="ExternalInput").ap()
    w2 = nc.dram_tensor("w2", [E, DM, HID], F32, kind="ExternalInput").ap()
    w3 = nc.dram_tensor("w3", [E, HID, DM], F32, kind="ExternalInput").ap()
    ident_in = nc.dram_tensor("ident_in", [128, 128], F32, kind="ExternalInput").ap()
    xo = nc.dram_tensor("xo", [DM, TOW], F32, kind="ExternalOutput").ap()

    t = lambda *a: st.enter_context(nc.sbuf_tensor(*a))
    xr = [t(f"xr{c}", [128, TOW], F32) for c in range(NCH)]
    xb = [t(f"xb{c}", [128, TOW], BF16) for c in range(NCH)]
    a_bf = [t(f"a{c}", [128, 512], BF16) for c in range(16)]
    h2_bf = [t(f"h2{c}", [128, 512], BF16) for c in range(16)]
    yacc = [t(f"y{c}", [128, 512], F32) for c in range(NCH)]
    wg_sb = t("wg_sb", [128, 64], F32)
    w32 = t("w32", [128, 2048], F32)
    wbf = t("wbf", [128, 2048], BF16)
    identf = t("identf", [128, 128], F32)
    g_sb = t("g_sb", [8, 512], F32)
    gate_tok = t("gate_tok", [128, 32], F32)
    m1 = t("m1", [128, 1], F32)
    m2 = t("m2", [128, 1], F32)
    p1 = t("p1", [128, 1], F32)
    p2 = t("p2", [128, 1], F32)
    eq1 = t("eq1", [128, 8], F32)
    eq2 = t("eq2", [128, 8], F32)
    g2t = t("g2t", [128, 8], F32)
    wcomb = t("wcomb", [128, 32], F32)
    wrow1 = t("wrow1", [1, 512], F32)
    wrow_all = t("wrow_all", [128, 8 * 512], F32)
    lnx = t("lnx", [128, 512], BF16)
    sq_bf = t("sq_bf", [128, 512], BF16)
    ones_bf = t("ones_bf", [128, 1], BF16)
    ones_row = t("ones_row", [1, 128], F32)
    stat = t("stat", [1, 3 * 512], F32)
    stat2 = t("stat2", [1, 2 * 512], F32)
    out_f = t("out_f", [128, 512], F32)

    ps = st.enter_context(nc.psum_tensor("ps", [128, 2 * 512], F32))
    mm_ps = [ps.ap()[:, 0:512], ps.ap()[:, 512:1024]]
    ps2 = st.enter_context(nc.psum_tensor("ps2", [128, 2 * 512], F32))
    mm2_ps = [ps2.ap()[:, 0:512], ps2.ap()[:, 512:1024]]
    gt_ps_t = st.enter_context(nc.psum_tensor("gt_ps", [128, 512], F32))
    gt_ps = gt_ps_t.ap()
    st_ps = st.enter_context(nc.psum_tensor("st_ps", [1, 512], F32))
    st2_ps = st.enter_context(nc.psum_tensor("st2_ps", [1, 512], F32))
    bc_ps = mm_ps[0]
    bc2_ps = mm_ps[1]

    ld = Sem(st, nc, "ld")
    sp = Sem(st, nc, "sp")
    sv = Sem(st, nc, "sv")
    sa = Sem(st, nc, "sa")
    so = Sem(st, nc, "so")

    p = Prog(nc)

    p.dma("sync", ld, identf[:, :], ident_in)
    p.dma("sync", ld, wg_sb[:, 0:64],
          AP(wg.tensor, 0, [[8, 128], [128 * 8, 8], [1, 8]]))
    for c in range(NCH):
        p.dma("sync", ld, xr[c][:, :], xf[c * 128:(c + 1) * 128, :])
        def cast(e, d=xb[c][:, :], s=xr[c][:, :]):
            return e.tensor_copy(d, s)
        p.emit("gpsimd", cast, waits=[(ld.h, ld.v)], inc=so.inc())
    def ones_e(e, d=ones_row[0:1, :]):
        return e.memset(d, 1.0)
    p.emit("vector", ones_e, inc=sv.inc())
    p.flush("load_x")

    NTT = 4
    for tt in range(NTT):
        tsl = slice(tt * 512, (tt + 1) * 512)
        for k in range(NCH):
            def mm(e, pa=gt_ps[0:8, :], w=wg_sb[:, k * 8:(k + 1) * 8],
                   r=xr[k][:, tsl], st_=(k == 0), sp_=(k == NCH - 1)):
                return e.matmul(pa, w, r, start=st_, stop=sp_)
            p.emit("tensor", mm, waits=([(sv.h, sv.v), (sa.h, sa.v)] if k == 0 else []),
                   inc=sp.inc())
        def gev(e, d=g_sb[:, :], s=gt_ps[0:8, :]):
            return e.tensor_copy(d, s)
        p.emit("vector", gev, waits=[(sp.h, sp.v)], inc=sv.inc())
        for gr in range(4):
            def gtr(e, pa=gt_ps[:, 0:8], s=g_sb[:, gr * 128:(gr + 1) * 128],
                    idt=identf[0:8, 0:8]):
                return e.transpose(pa, s, idt)
            p.emit("tensor", gtr, waits=[(sv.h, sv.v)], inc=sp.inc())
            def gtev(e, d=gate_tok[:, gr * 8:(gr + 1) * 8], s=gt_ps[:, 0:8]):
                return e.tensor_copy(d, s)
            p.emit("vector", gtev, waits=[(sp.h, sp.v)], inc=sv.inc())
        for gr in range(4):
            gsl = gate_tok[:, gr * 8:(gr + 1) * 8]
            wsl = wcomb[:, gr * 8:(gr + 1) * 8]
            def t1(e, d=m1[:, :], s=gsl):
                return e.tensor_reduce(d, s, AX, ALU.max)
            p.emit("vector", t1, waits=[(sv.h, sv.v)], inc=sv.inc())
            def t2(e, d=eq1[:, :], s=gsl, m=m1[:, :]):
                return e.tensor_scalar(d, s, m, None, ALU.is_equal)
            p.emit("vector", t2, waits=[(sv.h, sv.v)], inc=sv.inc())
            def t3(e, d=g2t[:, :], s=eq1[:, :], g=gsl):
                return e.scalar_tensor_tensor(d, s, NEG, g, ALU.mult, ALU.add)
            p.emit("vector", t3, waits=[(sv.h, sv.v)], inc=sv.inc())
            def t4(e, d=m2[:, :], s=g2t[:, :]):
                return e.tensor_reduce(d, s, AX, ALU.max)
            p.emit("vector", t4, waits=[(sv.h, sv.v)], inc=sv.inc())
            def t5(e, d=eq2[:, :], s=g2t[:, :], m=m2[:, :]):
                return e.tensor_scalar(d, s, m, None, ALU.is_equal)
            p.emit("vector", t5, waits=[(sv.h, sv.v)], inc=sv.inc())
            def t6(e, d=p1[:, :], a=m1[:, :], b=m2[:, :]):
                return e.tensor_sub(d, a, b)
            p.emit("vector", t6, waits=[(sv.h, sv.v)], inc=sv.inc())
            def t7(e, d=p1[:, :], s=p1[:, :]):
                return e.activation(d, s, AF.Sigmoid)
            p.emit("scalar", t7, waits=[(sv.h, sv.v)], inc=sa.inc())
            def t8(e, d=p2[:, :], s=p1[:, :]):
                return e.tensor_scalar(d, s, -1.0, 1.0, ALU.mult, ALU.add)
            p.emit("vector", t8, waits=[(sa.h, sa.v)], inc=sv.inc())
            def t9(e, d=wsl, s=eq1[:, :], m=p1[:, :]):
                return e.tensor_scalar(d, s, m, None, ALU.mult)
            p.emit("vector", t9, waits=[(sv.h, sv.v)], inc=sv.inc())
            def t10(e, d=wsl, s=eq2[:, :], m=p2[:, :], w=wsl):
                return e.scalar_tensor_tensor(d, s, m, w, ALU.mult, ALU.add)
            p.emit("vector", t10, waits=[(sv.h, sv.v)], inc=sv.inc())
        for e_ in range(E):
            for gr in range(4):
                def wtr(e, pa=gt_ps[0:1, 0:128], s=wcomb[:, gr * 8 + e_:gr * 8 + e_ + 1],
                        idt=identf[:, :]):
                    return e.transpose(pa, s, idt)
                p.emit("tensor", wtr, waits=[(sv.h, sv.v)], inc=sp.inc())
                def wtev(e, d=wrow1[0:1, gr * 128:(gr + 1) * 128], s=gt_ps[0:1, 0:128]):
                    return e.tensor_copy(d, s)
                p.emit("vector", wtev, waits=[(sp.h, sp.v)], inc=sv.inc())
            def wb(e, pa=gt_ps[:, :], l=ones_row[0:1, :], r=wrow1[0:1, :]):
                return e.matmul(pa, l, r, start=True, stop=True)
            p.emit("tensor", wb, waits=[(sv.h, sv.v)], inc=sp.inc())
            def wbev(e, d=wrow_all[:, e_ * 512:(e_ + 1) * 512], s=gt_ps[:, :]):
                return e.tensor_copy(d, s)
            p.emit("vector", wbev, waits=[(sp.h, sp.v)], inc=sv.inc())
        p.flush(f"gate_{tt}")

        for e_ in range(E):
            for role in range(2):
                wsrc = w1 if role == 0 else w2
                for ocpair in range(8):
                    load_weight_cols(p, [(sp.h, sp.v), (so.h, so.v)], ld,
                                     AP(wsrc.tensor, e_ * DM * HID,
                                        [[HID, DM], [1, HID]]),
                                     DM, ocpair * 256, 256, w32, wbf, so,
                                     extra_cast_waits=[(sp.h, sp.v)])
                    for half in range(2):
                        oc = ocpair * 2 + half
                        dst = (mm_ps if role == 0 else mm2_ps)[half]
                        for k in range(NCH):
                            def mm(e, pa=dst, w=wbf[:, k * 256 + half * 128:
                                                    k * 256 + half * 128 + 128],
                                   r=xb[k][:, tsl], st_=(k == 0), sp_=(k == NCH - 1)):
                                return e.matmul(pa, w, r, start=st_, stop=sp_)
                            p.emit("tensor", mm,
                                   waits=([(so.h, so.v), (sv.h, sv.v), (sa.h, sa.v)]
                                          if k == 0 else []),
                                   inc=sp.inc() if k == NCH - 1 else None)
                        if role == 0:
                            def e1(e, d=a_bf[oc][:, :], s=dst):
                                return e.activation(d, s, AF.Silu)
                            p.emit("scalar", e1, waits=[(sp.h, sp.v)], inc=sa.inc())
                        else:
                            def e1(e, d=h2_bf[oc][:, :], s=dst,
                                   w=wrow_all[:, e_ * 512:(e_ + 1) * 512]):
                                return e.tensor_mul(d, s, w)
                            p.emit("vector", e1, waits=[(sp.h, sp.v)], inc=sv.inc())
            for oc in range(16):
                def e2(e, d=a_bf[oc][:, :], a=a_bf[oc][:, :], b=h2_bf[oc][:, :]):
                    return e.tensor_mul(d, a, b)
                p.emit("vector", e2, waits=[(sv.h, sv.v), (sa.h, sa.v)], inc=sv.inc())
            for oc in range(NCH):
                load_weight_cols(p, [(sp.h, sp.v), (so.h, so.v)], ld,
                                 AP(w3.tensor, e_ * HID * DM,
                                    [[DM, HID], [1, DM]]),
                                 HID, oc * 128, 128, w32, wbf, so,
                                 extra_cast_waits=[(sp.h, sp.v)])
                for k in range(16):
                    def mm(e, pa=mm_ps[oc % 2], w=wbf[:, k * 128:(k + 1) * 128],
                           r=a_bf[k][:, :], st_=(k == 0), sp_=(k == 15)):
                        return e.matmul(pa, w, r, start=st_, stop=sp_)
                    p.emit("tensor", mm,
                           waits=([(so.h, so.v), (sv.h, sv.v)] if k == 0 else []),
                           inc=sp.inc() if k == 15 else None)
                if e_ == 0:
                    def e3(e, d=yacc[oc][:, :], s=mm_ps[oc % 2]):
                        return e.tensor_copy(d, s)
                else:
                    def e3(e, d=yacc[oc][:, :], a=yacc[oc][:, :], s=mm_ps[oc % 2]):
                        return e.tensor_add(d, a, s)
                p.emit("vector", e3, waits=[(sp.h, sp.v), (sv.h, sv.v)], inc=sv.inc())
        p.flush(f"exp_{tt}")
        for oc in range(NCH):
            def e4(e, d=xr[oc][:, tsl], a=xr[oc][:, tsl], b=yacc[oc][:, :]):
                return e.tensor_add(d, a, b)
            p.emit("vector", e4, waits=[(sv.h, sv.v)], inc=sv.inc())
        p.flush(f"res_{tt}")

    def ones_set(e, d=ones_bf[:, :]):
        return e.memset(d, 1.0)
    p.emit("vector", ones_set, inc=sv.inc())
    def ones_set2(e, d=ones_row[0:1, :]):
        return e.memset(d, 1.0)
    p.emit("vector", ones_set2, inc=sv.inc())
    emit_ln(p, nc, xr, None, None, TOW, lnx, sq_bf, (ones_bf, ones_row), stat,
            stat2, st_ps, st2_ps, bc_ps, bc2_ps, out_f, xo, ld, sp, sv, sa, so,
            1e-5)
    p.flush("ln")
    def fin(e):
        e.wait_ge(ld.h, ld.v)
        return None
    p.emit("sync", fin)
    p.flush("fin")
    st.close()
    return nc


def run_l1_ffn(x_fm, ins):
    extras = {
        "wg": np.ascontiguousarray(ins["L1_ffn_Wg"].T).astype(np.float32),
        "w1": np.ascontiguousarray(ins["L1_ffn_W1"].transpose(0, 2, 1)).astype(np.float32),
        "w2": np.ascontiguousarray(ins["L1_ffn_W2"].transpose(0, 2, 1)).astype(np.float32),
        "w3": np.ascontiguousarray(ins["L1_ffn_W3"].transpose(0, 2, 1)).astype(np.float32),
        "ident_in": np.eye(128, dtype=np.float32),
    }
    nc = build_l1_ffn()
    in_maps = []
    for core in range(8):
        b, half = core // 2, core % 2
        im = {"xf": np.ascontiguousarray(x_fm[b, :, half * TOW:(half + 1) * TOW])}
        im.update(extras)
        in_maps.append(im)
    res = run_bass_kernel_spmd(nc, in_maps, core_ids=list(range(8)))
    out = np.empty((NB, DM, SEQ), np.float32)
    for core in range(8):
        b, half = core // 2, core % 2
        out[b, :, half * TOW:(half + 1) * TOW] = res.results[core]["xo"]
    return out



def build_l3_ffn():
    nc = bass.Bass("TRN2", target_bir_lowering=False)
    import contextlib
    st = contextlib.ExitStack()

    E, F, FS = 16, 512, 1024
    xf = nc.dram_tensor("xf", [DM, TOW], F32, kind="ExternalInput").ap()
    wgt = nc.dram_tensor("wgt", [DM, E], F32, kind="ExternalInput").ap()
    w1e = nc.dram_tensor("w1e", [E, DM, F], F32, kind="ExternalInput").ap()
    w3e = nc.dram_tensor("w3e", [E, DM, F], F32, kind="ExternalInput").ap()
    w2e = nc.dram_tensor("w2e", [E, F, DM], F32, kind="ExternalInput").ap()
    w1s = nc.dram_tensor("w1s", [DM, FS], F32, kind="ExternalInput").ap()
    w3s = nc.dram_tensor("w3s", [DM, FS], F32, kind="ExternalInput").ap()
    w2s = nc.dram_tensor("w2s", [FS, DM], F32, kind="ExternalInput").ap()
    ident_in = nc.dram_tensor("ident_in", [128, 128], F32, kind="ExternalInput").ap()
    xo = nc.dram_tensor("xo", [DM, TOW], F32, kind="ExternalOutput").ap()

    t = lambda *a: st.enter_context(nc.sbuf_tensor(*a))
    xr = [t(f"xr{c}", [128, TOW], F32) for c in range(NCH)]
    xb = [t(f"xb{c}", [128, TOW], BF16) for c in range(NCH)]
    a_bf = [t(f"a{c}", [128, 512], BF16) for c in range(8)]
    h2_bf = [t(f"h2{c}", [128, 512], BF16) for c in range(8)]
    yacc = [t(f"y{c}", [128, 512], F32) for c in range(NCH)]
    wg_sb = t("wg_sb", [128, 128], F32)
    w32 = t("w32", [128, 2048], F32)
    wbf = t("wbf", [128, 2048], BF16)
    identf = t("identf", [128, 128], F32)
    g_sb = t("g_sb", [16, 512], F32)
    gate_tok = t("gate_tok", [128, 64], F32)
    ind = t("ind", [128, 16], F32)
    smv = t("smv", [128, 16], F32)
    al16 = t("al16", [128, 16], F32)
    gsum = t("gsum", [128, 4], F32)
    m1 = t("m1", [128, 1], F32)
    m2 = t("m2", [128, 1], F32)
    eqt = t("eqt", [128, 16], F32)
    rs = t("rs", [128, 1], F32)
    wcomb = t("wcomb", [128, 64], F32)
    wrow1 = t("wrow1", [1, 512], F32)
    wrow_all = t("wrow_all", [128, 16 * 512], F32)
    lnx = t("lnx", [128, 512], BF16)
    sq_bf = t("sq_bf", [128, 512], BF16)
    ones_bf = t("ones_bf", [128, 1], BF16)
    ones_row = t("ones_row", [1, 128], F32)
    stat = t("stat", [1, 3 * 512], F32)
    stat2 = t("stat2", [1, 2 * 512], F32)
    out_f = t("out_f", [128, 512], F32)

    ps = st.enter_context(nc.psum_tensor("ps", [128, 2 * 512], F32))
    mm_ps = [ps.ap()[:, 0:512], ps.ap()[:, 512:1024]]
    ps2 = st.enter_context(nc.psum_tensor("ps2", [128, 2 * 512], F32))
    mm2_ps = [ps2.ap()[:, 0:512], ps2.ap()[:, 512:1024]]
    gt_ps_t = st.enter_context(nc.psum_tensor("gt_ps", [128, 512], F32))
    gt_ps = gt_ps_t.ap()
    st_ps = st.enter_context(nc.psum_tensor("st_ps", [1, 512], F32))
    st2_ps = st.enter_context(nc.psum_tensor("st2_ps", [1, 512], F32))
    bc_ps = mm_ps[0]
    bc2_ps = mm_ps[1]

    ld = Sem(st, nc, "ld")
    sp = Sem(st, nc, "sp")
    sv = Sem(st, nc, "sv")
    sa = Sem(st, nc, "sa")
    so = Sem(st, nc, "so")

    p = Prog(nc)

    p.dma("sync", ld, identf[:, :], ident_in)
    p.dma("sync", ld, wg_sb[:, 0:128],
          AP(wgt.tensor, 0, [[16, 128], [128 * 16, 8], [1, 16]]))
    for c in range(NCH):
        p.dma("sync", ld, xr[c][:, :], xf[c * 128:(c + 1) * 128, :])
        def cast(e, d=xb[c][:, :], s=xr[c][:, :]):
            return e.tensor_copy(d, s)
        p.emit("gpsimd", cast, waits=[(ld.h, ld.v)], inc=so.inc())
    def ones_e(e, d=ones_row[0:1, :]):
        return e.memset(d, 1.0)
    p.emit("vector", ones_e, inc=sv.inc())
    p.flush("load_x")

    NTT = 4
    for tt in range(NTT):
        tsl = slice(tt * 512, (tt + 1) * 512)
        for k in range(NCH):
            def mm(e, pa=gt_ps[0:16, :], w=wg_sb[:, k * 16:(k + 1) * 16],
                   r=xr[k][:, tsl], st_=(k == 0), sp_=(k == NCH - 1)):
                return e.matmul(pa, w, r, start=st_, stop=sp_)
            p.emit("tensor", mm, waits=([(sv.h, sv.v), (sa.h, sa.v)] if k == 0 else []),
                   inc=sp.inc())
        def gev(e, d=g_sb[:, :], s=gt_ps[0:16, :]):
            return e.activation(d, s, AF.Sigmoid)
        p.emit("scalar", gev, waits=[(sp.h, sp.v)], inc=sa.inc())
        for gr in range(4):
            def gtr(e, pa=gt_ps[:, 0:16], s=g_sb[:, gr * 128:(gr + 1) * 128],
                    idt=identf[0:16, 0:16]):
                return e.transpose(pa, s, idt)
            p.emit("tensor", gtr, waits=[(sa.h, sa.v), (sv.h, sv.v)], inc=sp.inc())
            def gtev(e, d=gate_tok[:, gr * 16:(gr + 1) * 16], s=gt_ps[:, 0:16]):
                return e.tensor_copy(d, s)
            p.emit("vector", gtev, waits=[(sp.h, sp.v)], inc=sv.inc())
        for gr in range(4):
            ssl = gate_tok[:, gr * 16:(gr + 1) * 16]
            wsl = wcomb[:, gr * 16:(gr + 1) * 16]
            def V(fn, waits=None):
                p.emit("vector", fn,
                       waits=(waits if waits is not None else [(sv.h, sv.v)]),
                       inc=sv.inc())
            for g4 in range(4):
                gslc = ssl[:, g4 * 4:(g4 + 1) * 4]
                V(lambda e, d=m1[:, :], s=gslc: e.tensor_reduce(d, s, AX, ALU.max))
                V(lambda e, d=eqt[:, 0:4], s=gslc, m=m1[:, :]:
                  e.tensor_scalar(d, s, m, None, ALU.is_equal))
                V(lambda e, d=smv[:, 0:4], s=eqt[:, 0:4], g=gslc:
                  e.scalar_tensor_tensor(d, s, NEG, g, ALU.mult, ALU.add))
                V(lambda e, d=m2[:, :], s=smv[:, 0:4]:
                  e.tensor_reduce(d, s, AX, ALU.max))
                V(lambda e, d=gsum[:, g4:g4 + 1], a=m1[:, :], b=m2[:, :]:
                  e.tensor_add(d, a, b))
            V(lambda e, d=m1[:, :], s=gsum[:, :]: e.tensor_reduce(d, s, AX, ALU.max))
            V(lambda e, d=eqt[:, 0:4], s=gsum[:, :], m=m1[:, :]:
              e.tensor_scalar(d, s, m, None, ALU.is_equal))
            V(lambda e, d=smv[:, 0:4], s=eqt[:, 0:4], g=gsum[:, :]:
              e.scalar_tensor_tensor(d, s, NEG, g, ALU.mult, ALU.add))
            V(lambda e, d=m2[:, :], s=smv[:, 0:4]:
              e.tensor_reduce(d, s, AX, ALU.max))
            V(lambda e, d=eqt[:, 4:8], s=smv[:, 0:4], m=m2[:, :]:
              e.tensor_scalar(d, s, m, None, ALU.is_equal))
            V(lambda e, d=eqt[:, 0:4], a=eqt[:, 0:4], b=eqt[:, 4:8]:
              e.tensor_add(d, a, b))
            for g4 in range(4):
                for j in range(4):
                    V(lambda e, d=al16[:, g4 * 4 + j:g4 * 4 + j + 1],
                      s=eqt[:, g4:g4 + 1]: e.tensor_copy(d, s))
            V(lambda e, d=al16[:, :], s=al16[:, :]:
              e.tensor_scalar(d, s, -1.0, 1.0, ALU.mult, ALU.add))
            V(lambda e, d=smv[:, :], s=al16[:, :], g=ssl:
              e.scalar_tensor_tensor(d, s, NEG, g, ALU.mult, ALU.add))
            V(lambda e, d=ind[:, :]: e.memset(d, 0.0))
            for it in range(4):
                V(lambda e, d=m1[:, :], s=smv[:, :]:
                  e.tensor_reduce(d, s, AX, ALU.max))
                V(lambda e, d=eqt[:, :], s=smv[:, :], m=m1[:, :]:
                  e.tensor_scalar(d, s, m, None, ALU.is_equal))
                V(lambda e, d=ind[:, :], a=ind[:, :], b=eqt[:, :]:
                  e.tensor_add(d, a, b))
                V(lambda e, d=smv[:, :], s=eqt[:, :], g=smv[:, :]:
                  e.scalar_tensor_tensor(d, s, NEG, g, ALU.mult, ALU.add))
            V(lambda e, d=wsl, a=ssl, b=ind[:, :]: e.tensor_mul(d, a, b))
            V(lambda e, d=rs[:, :], s=wsl: e.tensor_reduce(d, s, AX, ALU.add))
            V(lambda e, d=rs[:, :], s=rs[:, :]:
              e.tensor_scalar(d, s, 1e-6, None, ALU.add))
            V(lambda e, d=rs[:, :], s=rs[:, :]: e.reciprocal(d, s))
            V(lambda e, d=wsl, s=wsl, m=rs[:, :]:
              e.tensor_scalar(d, s, m, None, ALU.mult))
        for e_ in range(E):
            for gr in range(4):
                def wtr(e, pa=gt_ps[0:1, 0:128],
                        s=wcomb[:, gr * 16 + e_:gr * 16 + e_ + 1],
                        idt=identf[:, :]):
                    return e.transpose(pa, s, idt)
                p.emit("tensor", wtr, waits=[(sv.h, sv.v)], inc=sp.inc())
                def wtev(e, d=wrow1[0:1, gr * 128:(gr + 1) * 128], s=gt_ps[0:1, 0:128]):
                    return e.tensor_copy(d, s)
                p.emit("vector", wtev, waits=[(sp.h, sp.v)], inc=sv.inc())
            def wb(e, pa=gt_ps[:, :], l=ones_row[0:1, :], r=wrow1[0:1, :]):
                return e.matmul(pa, l, r, start=True, stop=True)
            p.emit("tensor", wb, waits=[(sv.h, sv.v)], inc=sp.inc())
            def wbev(e, d=wrow_all[:, e_ * 512:(e_ + 1) * 512], s=gt_ps[:, :]):
                return e.tensor_copy(d, s)
            p.emit("vector", wbev, waits=[(sp.h, sp.v)], inc=sv.inc())
        p.flush(f"gate_{tt}")

        for role, wsrc in ((0, w1s), (1, w3s)):
            for ocpair in range(4):
                load_weight_cols(p, [(sp.h, sp.v), (so.h, so.v)], ld, wsrc, DM,
                                 ocpair * 256, 256, w32, wbf, so,
                                 extra_cast_waits=[(sp.h, sp.v)])
                for half in range(2):
                    oc = ocpair * 2 + half
                    dst = (mm_ps if role == 0 else mm2_ps)[half]
                    for k in range(NCH):
                        def mm(e, pa=dst, w=wbf[:, k * 256 + half * 128:
                                                k * 256 + half * 128 + 128],
                               r=xb[k][:, tsl], st_=(k == 0), sp_=(k == NCH - 1)):
                            return e.matmul(pa, w, r, start=st_, stop=sp_)
                        p.emit("tensor", mm,
                               waits=([(so.h, so.v), (sv.h, sv.v), (sa.h, sa.v)]
                                      if k == 0 else []),
                               inc=sp.inc() if k == NCH - 1 else None)
                    if role == 0:
                        def e1(e, d=a_bf[oc][:, :], s=dst):
                            return e.activation(d, s, AF.Silu)
                        p.emit("scalar", e1, waits=[(sp.h, sp.v)], inc=sa.inc())
                    else:
                        def e1(e, d=h2_bf[oc][:, :], s=dst):
                            return e.tensor_copy(d, s)
                        p.emit("vector", e1, waits=[(sp.h, sp.v)], inc=sv.inc())
        for oc in range(8):
            def e2(e, d=a_bf[oc][:, :], a=a_bf[oc][:, :], b=h2_bf[oc][:, :]):
                return e.tensor_mul(d, a, b)
            p.emit("vector", e2, waits=[(sv.h, sv.v), (sa.h, sa.v)], inc=sv.inc())
        for oc in range(NCH):
            load_weight_cols(p, [(sp.h, sp.v), (so.h, so.v)], ld, w2s, FS,
                             oc * 128, 128, w32, wbf, so,
                             extra_cast_waits=[(sp.h, sp.v)])
            for k in range(8):
                def mm(e, pa=mm_ps[oc % 2], w=wbf[:, k * 128:(k + 1) * 128],
                       r=a_bf[k][:, :], st_=(k == 0), sp_=(k == 7)):
                    return e.matmul(pa, w, r, start=st_, stop=sp_)
                p.emit("tensor", mm,
                       waits=([(so.h, so.v), (sv.h, sv.v)] if k == 0 else []),
                       inc=sp.inc() if k == 7 else None)
            def e3(e, d=yacc[oc][:, :], s=mm_ps[oc % 2]):
                return e.tensor_copy(d, s)
            p.emit("vector", e3, waits=[(sp.h, sp.v), (sv.h, sv.v)], inc=sv.inc())
        p.flush(f"shared_{tt}")

        for e_ in range(E):
            for role in range(2):
                wsrc = w1e if role == 0 else w3e
                for ocpair in range(2):
                    load_weight_cols(p, [(sp.h, sp.v), (so.h, so.v)], ld,
                                     AP(wsrc.tensor, e_ * DM * F, [[F, DM], [1, F]]),
                                     DM, ocpair * 256, 256, w32, wbf, so,
                                     extra_cast_waits=[(sp.h, sp.v)])
                    for half in range(2):
                        oc = ocpair * 2 + half
                        dst = (mm_ps if role == 0 else mm2_ps)[half]
                        for k in range(NCH):
                            def mm(e, pa=dst, w=wbf[:, k * 256 + half * 128:
                                                    k * 256 + half * 128 + 128],
                                   r=xb[k][:, tsl], st_=(k == 0), sp_=(k == NCH - 1)):
                                return e.matmul(pa, w, r, start=st_, stop=sp_)
                            p.emit("tensor", mm,
                                   waits=([(so.h, so.v), (sv.h, sv.v), (sa.h, sa.v)]
                                          if k == 0 else []),
                                   inc=sp.inc() if k == NCH - 1 else None)
                        if role == 0:
                            def e1(e, d=a_bf[oc][:, :], s=dst):
                                return e.activation(d, s, AF.Silu)
                            p.emit("scalar", e1, waits=[(sp.h, sp.v)], inc=sa.inc())
                        else:
                            def e1(e, d=h2_bf[oc][:, :], s=dst,
                                   w=wrow_all[:, e_ * 512:(e_ + 1) * 512]):
                                return e.tensor_mul(d, s, w)
                            p.emit("vector", e1, waits=[(sp.h, sp.v)], inc=sv.inc())
            for oc in range(4):
                def e2(e, d=a_bf[oc][:, :], a=a_bf[oc][:, :], b=h2_bf[oc][:, :]):
                    return e.tensor_mul(d, a, b)
                p.emit("vector", e2, waits=[(sv.h, sv.v), (sa.h, sa.v)], inc=sv.inc())
            for oc in range(NCH):
                load_weight_cols(p, [(sp.h, sp.v), (so.h, so.v)], ld,
                                 AP(w2e.tensor, e_ * F * DM, [[DM, F], [1, DM]]),
                                 F, oc * 128, 128, w32, wbf, so,
                                 extra_cast_waits=[(sp.h, sp.v)])
                for k in range(4):
                    def mm(e, pa=mm_ps[oc % 2], w=wbf[:, k * 128:(k + 1) * 128],
                           r=a_bf[k][:, :], st_=(k == 0), sp_=(k == 3)):
                        return e.matmul(pa, w, r, start=st_, stop=sp_)
                    p.emit("tensor", mm,
                           waits=([(so.h, so.v), (sv.h, sv.v)] if k == 0 else []),
                           inc=sp.inc() if k == 3 else None)
                def e3(e, d=yacc[oc][:, :], a=yacc[oc][:, :], s=mm_ps[oc % 2]):
                    return e.tensor_add(d, a, s)
                p.emit("vector", e3, waits=[(sp.h, sp.v), (sv.h, sv.v)], inc=sv.inc())
        p.flush(f"exp_{tt}")
        for oc in range(NCH):
            def e4(e, d=xr[oc][:, tsl], a=xr[oc][:, tsl], b=yacc[oc][:, :]):
                return e.tensor_add(d, a, b)
            p.emit("vector", e4, waits=[(sv.h, sv.v)], inc=sv.inc())
        p.flush(f"res_{tt}")

    def ones_set(e, d=ones_bf[:, :]):
        return e.memset(d, 1.0)
    p.emit("vector", ones_set, inc=sv.inc())
    emit_ln(p, nc, xr, None, None, TOW, lnx, sq_bf, (ones_bf, ones_row), stat,
            stat2, st_ps, st2_ps, bc_ps, bc2_ps, out_f, xo, ld, sp, sv, sa, so,
            1e-5)
    p.flush("ln")
    def fin(e):
        e.wait_ge(ld.h, ld.v)
        return None
    p.emit("sync", fin)
    p.flush("fin")
    st.close()
    return nc


def run_l3_ffn(x_fm, ins):
    extras = {
        "wgt": np.ascontiguousarray(ins["L3_ffn_w_gate"].T).astype(np.float32),
        "w1e": np.ascontiguousarray(ins["L3_ffn_w1_e"].transpose(0, 2, 1)).astype(np.float32),
        "w3e": np.ascontiguousarray(ins["L3_ffn_w3_e"].transpose(0, 2, 1)).astype(np.float32),
        "w2e": np.ascontiguousarray(ins["L3_ffn_w2_e"].transpose(0, 2, 1)).astype(np.float32),
        "w1s": np.ascontiguousarray(ins["L3_ffn_w1_s"].T).astype(np.float32),
        "w3s": np.ascontiguousarray(ins["L3_ffn_w3_s"].T).astype(np.float32),
        "w2s": np.ascontiguousarray(ins["L3_ffn_w2_s"].T).astype(np.float32),
        "ident_in": np.eye(128, dtype=np.float32),
    }
    nc = build_l3_ffn()
    in_maps = []
    for core in range(8):
        b, half = core // 2, core % 2
        im = {"xf": np.ascontiguousarray(x_fm[b, :, half * TOW:(half + 1) * TOW])}
        im.update(extras)
        in_maps.append(im)
    res = run_bass_kernel_spmd(nc, in_maps, core_ids=list(range(8)))
    out = np.empty((NB, DM, SEQ), np.float32)
    for core in range(8):
        b, half = core // 2, core % 2
        out[b, :, half * TOW:(half + 1) * TOW] = res.results[core]["xo"]
    return out



def build_l1_attn(t0, stage=9):
    nc = bass.Bass("TRN2", target_bir_lowering=False)
    import contextlib
    st = contextlib.ExitStack()

    xf = nc.dram_tensor("xf", [DM, SEQ], F32, kind="ExternalInput").ap()
    wq = nc.dram_tensor("wq", [DM, DM], F32, kind="ExternalInput").ap()
    wk = nc.dram_tensor("wk", [DM, DM], F32, kind="ExternalInput").ap()
    wv = nc.dram_tensor("wv", [DM, DM], F32, kind="ExternalInput").ap()
    uvc = nc.dram_tensor("uvc", [128, 16], F32, kind="ExternalInput").ap()
    rt2 = nc.dram_tensor("rt2", [128, SEQ], BF16, kind="ExternalInput").ap()
    wpr = nc.dram_tensor("wpr", [DM, DM], F32, kind="ExternalInput").ap()
    bpr = nc.dram_tensor("bpr", [128, 8], F32, kind="ExternalInput").ap()
    ident_in = nc.dram_tensor("ident_in", [128, 128], BF16, kind="ExternalInput").ap()
    xo = nc.dram_tensor("xo", [DM, TOW], F32, kind="ExternalOutput").ap()

    NXROW = TOW + 129
    xbuf = nc.dram_tensor("xbuf", [NXROW * 4097 + 8], F32).ap()

    t = lambda *a: st.enter_context(nc.sbuf_tensor(*a))
    xr = [t(f"xr{c}", [128, TOW], F32) for c in range(NCH)]
    xbf = [xr[c][:, :].bitcast(BF16) for c in range(NCH)]
    q1_bf = t("q1_bf", [128, TOW], BF16)
    q2_bf = t("q2_bf", [128, TOW + 128], BF16)
    k_bf = t("k_bf", [128, SEQ], BF16)
    v_fm = t("v_fm", [128, SEQ], BF16)
    vtok = t("vtok", [128, SEQ], BF16)
    rt_sb = t("rt_sb", [128, SEQ], BF16)
    o_bf = [t(f"o{c}", [128, TOW], BF16) for c in range(NCH)]
    s_row = t("s_row", [128, SEQ], F32)
    rx_buf = t("rx_buf", [128, 4097], F32)
    r_sb = rx_buf[:, 0:4096]
    xt_sb = rx_buf
    p_bf = t("p_bf", [128, SEQ], BF16)
    ident = t("ident", [128, 128], BF16)
    w32 = [t(f"w32_{i}", [128, 1024], F32) for i in range(2)]
    wbf = [t(f"wbf_{i}", [128, 1024], BF16) for i in range(2)]
    uv_sb = t("uv_sb", [128, 16], F32)
    bpr_sb = t("bpr_sb", [128, 8], F32)
    pt_sb = [t(f"pt{i}", [128, 128], BF16) for i in range(2)]
    mp = t("mp", [128, 8], F32)
    negm = t("negm", [128, 1], F32)
    den = t("den", [128, 1], F32)
    rden = t("rden", [128, 1], F32)
    sq_bf = t("sq_bf", [128, 512], BF16)
    ones_bf = t("ones_bf", [128, 1], BF16)
    ones_row = t("ones_row", [1, 128], F32)
    stat = t("stat", [1, 3 * 512], F32)
    stat2 = t("stat2", [1, 2 * 512], F32)
    out_f = t("out_f", [128, 512], F32)

    ps = st.enter_context(nc.psum_tensor("ps", [128, 2 * 512], F32))
    mm_ps = [ps.ap()[:, 0:512], ps.ap()[:, 512:1024]]
    pt_ps_t = st.enter_context(nc.psum_tensor("pt_ps", [128, 2 * 1024], BF16))
    pt_ps = [pt_ps_t.ap()[:, 0:128], pt_ps_t.ap()[:, 1024:1152]]
    out_ps = st.enter_context(nc.psum_tensor("out_ps", [64, 128], F32))
    st_ps = st.enter_context(nc.psum_tensor("st_ps", [1, 512], F32))
    st2_ps = st.enter_context(nc.psum_tensor("st2_ps", [1, 512], F32))
    bc_ps = mm_ps[0]
    bc2_ps = mm_ps[1]

    ld = Sem(st, nc, "ld")
    sp = Sem(st, nc, "sp")
    sv = Sem(st, nc, "sv")
    sa = Sem(st, nc, "sa")
    so = Sem(st, nc, "so")
    sx = Sem(st, nc, "sx")

    p = Prog(nc)

    p.dma("sync", ld, ident[:, :], ident_in)
    p.dma("sync", ld, uv_sb[:, :], uvc)
    p.dma("sync", ld, bpr_sb[:, :], bpr)
    p.dma("sync", ld, rt_sb[:, :], rt2)
    def z0(e, d=xt_sb[:, 0:1]):
        return e.memset(d, 0.0)
    p.emit("vector", z0, inc=sv.inc())
    def ones_e(e, d=ones_row[0:1, :]):
        return e.memset(d, 1.0)
    p.emit("vector", ones_e, inc=sv.inc())
    for c in range(NCH):
        for hh in range(2):
            dst = s_row[:, hh * 2048:(hh + 1) * 2048]
            srcd = xf[c * 128:(c + 1) * 128, hh * 2048:(hh + 1) * 2048]
            waits = [(sv.h, sv.v)] if (c + hh) else []
            p.dma("sync", ld, dst, srcd, waits=waits)
            def cast(e, d=xbf[c][:, hh * 2048:(hh + 1) * 2048], s=dst):
                return e.tensor_copy(d, s)
            p.emit("vector", cast, waits=[(ld.h, ld.v)], inc=sv.inc())
    p.flush("load_x")

    nqt_x = 17 if t0 == 0 else 16
    for c in range(NCH):
        specs = [("q", wq, c * 128), ("k", wk, c * 128), ("v", wv, c * 128)]
        for i, (role, wsrc, c0) in enumerate(specs):
            buf = i % 2
            load_weight_cols(p, [(sp.h, sp.v), (so.h, so.v)], ld, wsrc, DM, c0, 128,
                             w32[buf], wbf[buf], so,
                             extra_cast_waits=[(sp.h, sp.v)])
            if role == "q":
                ntt = 5 if t0 == 0 else 4
                toff = t0
            else:
                ntt = 8
                toff = 0
            for tt in range(ntt):
                slot = tt % 2
                tw = 512 if (role != "q" or tt < 4) else 128
                tsl = slice(toff + tt * 512, toff + tt * 512 + tw)
                lsl = slice(tt * 512, tt * 512 + tw)
                for k in range(NCH):
                    def mm(e, pa=mm_ps[slot][:, 0:tw],
                           w=wbf[buf][:, k * 128:(k + 1) * 128],
                           r=xbf[k][:, tsl], st_=(k == 0), sp_=(k == NCH - 1)):
                        return e.matmul(pa, w, r, start=st_, stop=sp_)
                    waits = [(so.h, so.v), (sv.h, sv.v)] if k == 0 else []
                    p.emit("tensor", mm, waits=waits,
                           inc=sp.inc() if k == NCH - 1 else None)
                if role == "q":
                    if tt < 4:
                        def ev(e, d=q1_bf[:, lsl], s=mm_ps[slot][:, 0:tw],
                               b=uv_sb[:, c:c + 1]):
                            return e.tensor_scalar(d, s, b, 0.125, ALU.add, ALU.mult)
                        p.emit("vector", ev, waits=[(sp.h, sp.v)], inc=sv.inc())
                    def ev2(e, d=q2_bf[:, lsl], s=mm_ps[slot][:, 0:tw],
                            b=uv_sb[:, 8 + c:9 + c]):
                        return e.tensor_scalar(d, s, b, 0.125, ALU.add, ALU.mult)
                    p.emit("vector", ev2, waits=[(sp.h, sp.v)], inc=sv.inc())
                elif role == "k":
                    def ev(e, d=k_bf[:, lsl], s=mm_ps[slot]):
                        return e.tensor_copy(d, s)
                    p.emit("vector", ev, waits=[(sp.h, sp.v)], inc=sv.inc())
                else:
                    def ev(e, d=v_fm[:, lsl], s=mm_ps[slot]):
                        return e.tensor_copy(d, s)
                    p.emit("vector", ev, waits=[(sp.h, sp.v)], inc=sv.inc())
        trev = {}
        for kc in range(SEQ // 128):
            slot = kc % 2
            def tr(e, pa=pt_ps[slot], s=v_fm[:, kc * 128:(kc + 1) * 128], idt=ident[:, :]):
                return e.transpose(pa, s, idt)
            waits = [(sv.h, sv.v)] if kc < 2 else [(sv.h, trev[kc - 2])]
            p.emit("tensor", tr, waits=waits, inc=sp.inc())
            def tev(e, d=vtok[:, kc * 128:(kc + 1) * 128], s=pt_ps[slot]):
                return e.tensor_copy(d, s)
            p.emit("vector", tev, waits=[(sp.h, sp.v)], inc=sv.inc())
            trev[kc] = sv.v
        p.flush(f"kv{c}")

        for hh in range(2):
            h = 2 * c + hh
            hb = 64 * hh
            q1_h = q1_bf[hb:hb + 64, :]
            q2_h = q2_bf[hb:hb + 64, :]
            k_h = k_bf[hb:hb + 64, :]
            rt_h = rt_sb[hb:hb + 64, :]
            def z1(e, d=xt_sb[:, 0:1]):
                return e.memset(d, 0.0)
            p.emit("vector", z1, waits=[(sv.h, sv.v), (sx.h, sx.v)], inc=sv.inc())
            for qt in range(nqt_x):
                for kt in range(8):
                    slot = kt % 2
                    def mm(e, pa=mm_ps[slot], l=q2_h[:, qt * 128:(qt + 1) * 128],
                           r=rt_h[:, kt * 512:(kt + 1) * 512]):
                        return e.matmul(pa, l, r, start=True, stop=True)
                    p.emit("tensor", mm, waits=[(sv.h, sv.v)], inc=sp.inc())
                    def ev(e, d=xt_sb[:, 1 + kt * 512:1 + (kt + 1) * 512],
                           s=mm_ps[slot]):
                        return e.tensor_copy(d, s)
                    p.emit("vector", ev, waits=[(sp.h, sp.v), (sx.h, sx.v)],
                           inc=sv.inc())
                def xd(e, d=AP(xbuf.tensor, qt * 128 * 4097,
                               [[4097, 128], [1, 4097]]), s=xt_sb[:, :]):
                    return e.dma_start(out=d, in_=s)
                p.emit("gpsimd", xd, waits=[(sv.h, sv.v), (sx.h, sx.v)],
                       inc=sx.inc(16))
            p.flush(f"x_{c}_{hh}")

            last_exp = sa.v
            last_tp = sp.v
            for qt in range(16):
                rbase = (qt * 128 + 1) * 4096 - t0
                def rd(e, d=r_sb[:, :], s=AP(xbuf.tensor, rbase,
                                             [[4096, 128], [1, 4096]])):
                    return e.dma_start(out=d, in_=s)
                p.emit("sync", rd, waits=[(sa.h, last_exp), (ld.h, ld.v),
                                          (sx.h, sx.v)], inc=ld.inc(16))
                for kt in range(8):
                    slot = kt % 2
                    def mm(e, pa=mm_ps[slot], l=q1_h[:, qt * 128:(qt + 1) * 128],
                           r=k_h[:, kt * 512:(kt + 1) * 512]):
                        return e.matmul(pa, l, r, start=True, stop=True)
                    waits = [(sv.h, evac_done[kt - 2])] if kt >= 2 else [(sv.h, sv.v)]
                    p.emit("tensor", mm, waits=waits, inc=sp.inc())
                    if kt == 0:
                        evac_done = {}
                    ssl = s_row[:, kt * 512:(kt + 1) * 512]
                    def ev1(e, d=ssl, s=mm_ps[slot],
                            rr=r_sb[:, kt * 512:(kt + 1) * 512]):
                        return e.scalar_tensor_tensor(d, s, 0.125, rr, ALU.bypass,
                                                      ALU.add)
                    p.emit("vector", ev1,
                           waits=[(sp.h, sp.v), (sa.h, last_exp), (ld.h, ld.v)],
                           inc=sv.inc())
                    def ev2(e, d=mp[:, kt:kt + 1], s=ssl):
                        return e.tensor_reduce(d, s, AX, ALU.max)
                    p.emit("vector", ev2, waits=[(sv.h, sv.v)], inc=sv.inc())
                    evac_done[kt] = sv.v
                def ng(e, d=negm[:, :], s=mp[:, 0:8]):
                    return e.tensor_reduce(d, s, AX, ALU.max, negate=True)
                p.emit("vector", ng, waits=[(sv.h, sv.v)], inc=sv.inc())
                def ex(e, d=p_bf[:, :], s=s_row[:, :], b=negm[:, :], a=den[:, :]):
                    return e.activation(d, s, AF.Exp, bias=b, scale=1.0, accum_out=a)
                p.emit("scalar", ex, waits=[(sv.h, sv.v), (sp.h, last_tp)],
                       inc=sa.inc())
                last_exp = sa.v
                def rc(e, d=rden[:, :], s=den[:, :]):
                    return e.reciprocal(d, s)
                p.emit("vector", rc, waits=[(sa.h, sa.v)], inc=sv.inc())
                def nm(e, d=p_bf[:, :], s=p_bf[:, :], r=rden[:, :]):
                    return e.tensor_scalar(d, s, r, None, ALU.mult)
                p.emit("vector", nm, waits=[(sv.h, sv.v)], inc=sv.inc())
                norm_done = sv.v
                ptev = {}
                for kc in range(32):
                    slot = kc % 2
                    def tr(e, pa=pt_ps[slot], s=p_bf[:, kc * 128:(kc + 1) * 128],
                           idt=ident[:, :]):
                        return e.transpose(pa, s, idt)
                    w = [(sv.h, norm_done)]
                    if kc >= 2:
                        w.append((sv.h, ptev[kc - 2]))
                    p.emit("tensor", tr, waits=w, inc=sp.inc())
                    def tev(e, d=pt_sb[slot][:, :], s=pt_ps[slot]):
                        return e.tensor_copy(d, s)
                    p.emit("vector", tev, waits=[(sp.h, sp.v)], inc=sv.inc())
                    ptev[kc] = sv.v
                    def vm(e, pa=out_ps.ap()[:, :],
                           l=vtok[:, kc * 128 + hb:kc * 128 + hb + 64],
                           r=pt_sb[slot][:, :], st_=(kc == 0), sp_=(kc == 31)):
                        return e.matmul(pa, l, r, start=st_, stop=sp_)
                    w = [(sv.h, ptev[kc])]
                    if kc == 0:
                        w.append((sa.h, sa.v))
                    p.emit("tensor", vm, waits=w, inc=sp.inc())
                last_tp = sp.v
                def oev(e, d=o_bf[c][hb:hb + 64, qt * 128:(qt + 1) * 128],
                        s=out_ps.ap()[:, :]):
                    return e.copy(d, s)
                p.emit("scalar", oev, waits=[(sp.h, sp.v)], inc=sa.inc())
            p.flush(f"attn{c}_{hh}")

    for c in range(NCH):
        p.dma("sync", ld, xr[c][:, :], xf[c * 128:(c + 1) * 128, t0:t0 + TOW])
    p.flush("reload_x")
    for oc in range(NCH):
        buf = oc % 2
        load_weight_cols(p, [(sp.h, sp.v), (so.h, so.v)], ld, wpr, DM, oc * 128, 128,
                         w32[buf], wbf[buf], so,
                         extra_cast_waits=[(sp.h, sp.v)])
        for tt in range(4):
            slot = tt % 2
            tsl = slice(tt * 512, (tt + 1) * 512)
            for k in range(NCH):
                def mm(e, pa=mm_ps[slot], w=wbf[buf][:, k * 128:(k + 1) * 128],
                       r=o_bf[k][:, tsl], st_=(k == 0), sp_=(k == NCH - 1)):
                    return e.matmul(pa, w, r, start=st_, stop=sp_)
                waits = [(so.h, so.v), (sv.h, sv.v)] if k == 0 else []
                p.emit("tensor", mm, waits=waits, inc=sp.inc() if k == NCH - 1 else None)
            def ev(e, d=xr[oc][:, tsl], s=mm_ps[slot],
                   b=bpr_sb[:, oc:oc + 1], x=xr[oc][:, tsl]):
                return e.scalar_tensor_tensor(d, s, b, x, ALU.add, ALU.add)
            p.emit("vector", ev, waits=[(sp.h, sp.v)], inc=sv.inc())
    p.flush("oproj")
    def ones_set(e, d=ones_bf[:, :]):
        return e.memset(d, 1.0)
    p.emit("vector", ones_set, inc=sv.inc())
    emit_ln(p, nc, xr, None, None, TOW, p_bf, sq_bf, (ones_bf, ones_row), stat,
            stat2, st_ps, st2_ps, bc_ps, bc2_ps, out_f, xo, ld, sp, sv, sa, so,
            1e-5)
    p.flush("ln")
    def fin(e):
        e.wait_ge(ld.h, ld.v)
        return None
    p.emit("sync", fin)
    p.flush("fin")
    st.close()
    return nc


def run_l1_attn(x_fm, ins):
    u = ins["L1_attn_u"].astype(np.float32)
    v = ins["L1_attn_v"].astype(np.float32)
    uv = np.zeros((128, 16), np.float32)
    for c in range(8):
        uv[0:64, c] = u[2 * c]
        uv[64:128, c] = u[2 * c + 1]
        uv[0:64, 8 + c] = v[2 * c]
        uv[64:128, 8 + c] = v[2 * c + 1]
    pos = np.arange(SEQ - 1, -1, -1, dtype=np.float32)
    inv_freq = 1.0 / 10000.0 ** (np.arange(0, 64, 2, dtype=np.float32) / 64)
    ang = pos[:, None] * inv_freq[None, :]
    r = np.concatenate([np.sin(ang), np.cos(ang)], axis=-1).astype(np.float32)
    rt2 = np.concatenate([r.T, r.T], axis=0).astype(BF)
    extras = {
        "wq": ins["L1_attn_Wq"].astype(np.float32),
        "wk": ins["L1_attn_Wk"].astype(np.float32),
        "wv": ins["L1_attn_Wv"].astype(np.float32),
        "uvc": uv, "rt2": np.ascontiguousarray(rt2),
        "wpr": ins["L1_attn_Wo"].astype(np.float32),
        "bpr": vec_chunks(ins["L1_attn_bo"]),
        "ident_in": np.eye(128, dtype=BF),
    }
    out = np.empty((NB, DM, SEQ), np.float32)
    for half in range(2):
        t0 = half * TOW
        nc = build_l1_attn(t0)
        in_maps = []
        cores = []
        for b in range(NB):
            cores.append(2 * b + half)
            im = {"xf": np.ascontiguousarray(x_fm[b])}
            im.update(extras)
            in_maps.append(im)
        res = run_bass_kernel_spmd(nc, in_maps, core_ids=cores)
        for i, b in enumerate(range(NB)):
            out[b, :, t0:t0 + TOW] = res.results[i]["xo"]
    return out



def _l0f_common(nc, st, t0):
    kstart = max(0, t0 - 512)
    klen = t0 + TOW - kstart
    return kstart, klen


def build_l0f_attn(t0):
    nc = bass.Bass("TRN2", target_bir_lowering=False)
    import contextlib
    st = contextlib.ExitStack()
    kstart, klen = _l0f_common(nc, st, t0)
    NKC = klen // 128
    INNER = 512

    xf = nc.dram_tensor("xf", [DM, SEQ], F32, kind="ExternalInput").ap()
    wqkv = nc.dram_tensor("wqkv", [DM, 3 * INNER], F32, kind="ExternalInput").ap()
    wo = nc.dram_tensor("wo", [INNER, DM], F32, kind="ExternalInput").ap()
    qks = nc.dram_tensor("qks", [128, 2], F32, kind="ExternalInput").ap()
    tabq = nc.dram_tensor("tabq", [128, 2 * TOW], F32, kind="ExternalInput").ap()
    tabk = nc.dram_tensor("tabk", [128, 2 * klen], F32, kind="ExternalInput").ap()
    bd2 = nc.dram_tensor("bd2", [128, 2], BF16, kind="ExternalInput").ap()
    bd2t = nc.dram_tensor("bd2t", [2, 128], F32, kind="ExternalInput").ap()
    maskt = nc.dram_tensor("maskt", [128, 1408], F32, kind="ExternalInput").ap()
    ident_in = nc.dram_tensor("ident_in", [128, 128], BF16, kind="ExternalInput").ap()
    xo = nc.dram_tensor("xo", [DM, TOW], F32, kind="ExternalOutput").ap()

    t = lambda *a: st.enter_context(nc.sbuf_tensor(*a))
    xw = [t(f"xw{c}", [128, 4098], BF16) for c in range(NCH)]
    xr = [xw[c][:, :].bitcast(F32)[:, 0:2048] for c in range(NCH)]
    qkf = t("qkf", [128, 2560], F32)
    rotf = t("rotf", [128, 2560], F32)
    q_bf = t("q_bf", [128, TOW], BF16)
    k_bf = t("k_bf", [128, 2560], BF16)
    v_fm = t("v_fm", [128, 2560], BF16)
    vtok = t("vtok", [128, 2560], BF16)
    tabq_sb = t("tabq_sb", [128, 2 * TOW], F32)
    tabk_sb = t("tabk_sb", [128, 2 * 2560], F32)
    o_bf = [t(f"o{c}", [128, TOW], BF16) for c in range(4)]
    s_row = t("s_row", [128, 1024], F32)
    p_bf = t("p_bf", [128, 1024], BF16)
    mask_sb = t("mask_sb", [128, 1408], F32)
    ident = t("ident", [128, 128], BF16)
    bd2_sb = t("bd2_sb", [128, 2], BF16)
    bd2t_sb = t("bd2t_sb", [2, 128], F32)
    qks_sb = t("qks_sb", [128, 2], F32)
    w32 = [t(f"w32_{i}", [128, 1024], F32) for i in range(2)]
    wbf = [t(f"wbf_{i}", [128, 1024], BF16) for i in range(2)]
    pt_sb = [t(f"pt{i}", [128, 128], BF16) for i in range(2)]
    mp = t("mp", [128, 2], F32)
    negm = t("negm", [128, 1], F32)
    den = t("den", [128, 1], F32)
    rden = t("rden", [128, 1], F32)
    rr2 = t("rr2", [2, 2560], F32)

    ps = st.enter_context(nc.psum_tensor("ps", [128, 2 * 512], F32))
    mm_ps = [ps.ap()[:, 0:512], ps.ap()[:, 512:1024]]
    pt_ps_t = st.enter_context(nc.psum_tensor("pt_ps", [128, 2 * 1024], BF16))
    pt_ps = [pt_ps_t.ap()[:, 0:128], pt_ps_t.ap()[:, 1024:1152]]
    out_ps = st.enter_context(nc.psum_tensor("out_ps", [64, 128], F32))
    nr_ps = st.enter_context(nc.psum_tensor("nr_ps", [2, 512], F32))
    bb_ps = st.enter_context(nc.psum_tensor("bb_ps", [128, 512], F32))

    ld = Sem(st, nc, "ld")
    sp = Sem(st, nc, "sp")
    sv = Sem(st, nc, "sv")
    sa = Sem(st, nc, "sa")
    so = Sem(st, nc, "so")
    sx = Sem(st, nc, "sx")

    p = Prog(nc)

    p.dma("sync", ld, mask_sb[:, :], maskt)
    p.dma("sync", ld, ident[:, :], ident_in)
    p.dma("sync", ld, bd2_sb[:, :], bd2)
    p.dma("sync", ld, bd2t_sb[:, :], bd2t)
    p.dma("sync", ld, qks_sb[:, :], qks)
    p.dma("sync", ld, tabq_sb[:, :], tabq)
    p.dma("sync", ld, tabk_sb[:, 0:2 * klen], tabk)
    for c in range(NCH):
        def zc(e, d=xw[c][:, 0:1]):
            return e.memset(d, 0.0)
        p.emit("vector", zc, inc=sv.inc())
        for hh in range(2):
            dst = qkf[:, 0:2048]
            srcd = xf[c * 128:(c + 1) * 128, hh * 2048:(hh + 1) * 2048]
            p.dma("sync", ld, dst, srcd, waits=[(sv.h, sv.v)])
            def cast(e, d=xw[c][:, 1 + hh * 2048:1 + (hh + 1) * 2048], s=dst):
                return e.tensor_copy(d, s)
            p.emit("vector", cast, waits=[(ld.h, ld.v)], inc=sv.inc())
    p.flush("load")

    for c in range(4):
        specs = [("q", c * 128, t0, TOW, q_bf, tabq_sb, 0, 8.0),
                 ("k", INNER + c * 128, kstart, klen, k_bf, tabk_sb, 1, 1.0),
                 ("v", 2 * INNER + c * 128, kstart, klen, None, None, 0, 1.0)]
        for i, (role, c0, toff, tlen, bqf, tab, scol, sc8) in enumerate(specs):
            buf = i % 2
            load_weight_cols(p, [(sp.h, sp.v), (so.h, so.v)], ld, wqkv, DM, c0, 128,
                             w32[buf], wbf[buf], so,
                             extra_cast_waits=[(sp.h, sp.v)])
            for tt in range(cdiv(tlen, 512)):
                slot = tt % 2
                tw = min(512, tlen - tt * 512)
                lsl = slice(tt * 512, tt * 512 + tw)
                for k in range(NCH):
                    coff = (1 + toff) if k < 4 else toff
                    def mm(e, pa=mm_ps[slot][:, 0:tw],
                           w=wbf[buf][:, k * 128:(k + 1) * 128],
                           r=xw[k][:, coff + tt * 512: coff + tt * 512 + tw],
                           st_=(k == 0), sp_=(k == NCH - 1)):
                        return e.matmul(pa, w, r, start=st_, stop=sp_)
                    waits = [(so.h, so.v), (sv.h, sv.v)] if k == 0 else []
                    p.emit("tensor", mm, waits=waits,
                           inc=sp.inc() if k == NCH - 1 else None)
                if role == "v":
                    def ev(e, d=v_fm[:, lsl], s=mm_ps[slot][:, 0:tw]):
                        return e.tensor_copy(d, s)
                    p.emit("vector", ev, waits=[(sp.h, sp.v)], inc=sv.inc())
                else:
                    def ev(e, d=qkf[:, lsl], s=mm_ps[slot][:, 0:tw]):
                        return e.tensor_copy(d, s)
                    p.emit("vector", ev, waits=[(sp.h, sp.v)], inc=sv.inc())
            if role == "v":
                continue
            for tt in range(cdiv(tlen, 512)):
                tw = min(512, tlen - tt * 512)
                lsl = slice(tt * 512, tt * 512 + tw)
                def sq(e, d=p_bf[:, 0:tw], s=qkf[:, lsl]):
                    return e.activation(d, s, AF.Square)
                p.emit("scalar", sq, waits=[(sv.h, sv.v), (sp.h, sp.v)], inc=sa.inc())
                def m1(e, pa=nr_ps.ap()[:, 0:tw], l=bd2_sb[:, :], r=p_bf[:, 0:tw]):
                    return e.matmul(pa, l, r, start=True, stop=True)
                p.emit("tensor", m1, waits=[(sa.h, sa.v)], inc=sp.inc())
                def r1(e, d=rr2[:, lsl], s=nr_ps.ap()[:, 0:tw]):
                    return e.tensor_scalar(d, s, 1e-24, None, ALU.max)
                p.emit("vector", r1, waits=[(sp.h, sp.v)], inc=sv.inc())
                def r2(e, d=rr2[:, lsl], s=rr2[:, lsl]):
                    return e.activation(d, s, AF.Ln)
                p.emit("scalar", r2, waits=[(sv.h, sv.v)], inc=sa.inc())
                def r3(e, d=rr2[:, lsl], s=rr2[:, lsl]):
                    return e.activation(d, s, AF.Exp, scale=-0.5)
                p.emit("scalar", r3, waits=[(sa.h, sa.v)], inc=sa.inc())
                def b1(e, pa=bb_ps.ap()[:, 0:tw], l=bd2t_sb[:, :], r=rr2[:, lsl]):
                    return e.matmul(pa, l, r, start=True, stop=True)
                p.emit("tensor", b1, waits=[(sa.h, sa.v)], inc=sp.inc())
                def ap1(e, d=qkf[:, lsl], s=qkf[:, lsl], sc=qks_sb[:, scol:scol + 1],
                        bb=bb_ps.ap()[:, 0:tw]):
                    return e.scalar_tensor_tensor(d, s, sc, bb, ALU.mult, ALU.mult)
                p.emit("vector", ap1, waits=[(sp.h, sp.v), (sa.h, sa.v)], inc=sv.inc())
            nn = sv.v
            def rd1(e, d=rotf[0:32, 0:tlen], s=qkf[32:64, 0:tlen]):
                return e.dma_start(out=d, in_=s)
            p.emit("gpsimd", rd1, waits=[(sv.h, nn), (sx.h, sx.v)], inc=sx.inc(16))
            def rd2(e, d=rotf[32:64, 0:tlen], s=qkf[0:32, 0:tlen]):
                return e.dma_start(out=d, in_=s)
            p.emit("gpsimd", rd2, waits=[(sx.h, sx.v)], inc=sx.inc(16))
            def rd3(e, d=rotf[64:96, 0:tlen], s=qkf[96:128, 0:tlen]):
                return e.dma_start(out=d, in_=s)
            p.emit("gpsimd", rd3, waits=[(sx.h, sx.v)], inc=sx.inc(16))
            def rd4(e, d=rotf[96:128, 0:tlen], s=qkf[64:96, 0:tlen]):
                return e.dma_start(out=d, in_=s)
            p.emit("gpsimd", rd4, waits=[(sx.h, sx.v)], inc=sx.inc(16))
            for tt in range(cdiv(tlen, 512)):
                tw = min(512, tlen - tt * 512)
                lsl = slice(tt * 512, tt * 512 + tw)
                def c1(e, d=qkf[:, lsl], a=qkf[:, lsl],
                       b=tab[:, tt * 512:tt * 512 + tw]):
                    return e.tensor_mul(d, a, b)
                p.emit("vector", c1, waits=[(sx.h, sx.v), (sv.h, sv.v)], inc=sv.inc())
                def c2(e, d=rotf[:, lsl], a=rotf[:, lsl],
                       b=tab[:, tlen + tt * 512: tlen + tt * 512 + tw]):
                    return e.tensor_mul(d, a, b)
                p.emit("vector", c2, waits=[(sv.h, sv.v)], inc=sv.inc())
                def c3(e, d=qkf[:, lsl], a=qkf[:, lsl], b=rotf[:, lsl]):
                    return e.tensor_add(d, a, b)
                p.emit("vector", c3, waits=[(sv.h, sv.v)], inc=sv.inc())
                def c4(e, d=bqf[:, lsl], s=qkf[:, lsl], m=sc8):
                    return e.tensor_scalar(d, s, m, None, ALU.mult)
                p.emit("vector", c4, waits=[(sv.h, sv.v)], inc=sv.inc())
        trev = {}
        for kc in range(NKC):
            slot = kc % 2
            def tr(e, pa=pt_ps[slot], s=v_fm[:, kc * 128:(kc + 1) * 128], idt=ident[:, :]):
                return e.transpose(pa, s, idt)
            waits = [(sv.h, sv.v)] if kc < 2 else [(sv.h, trev[kc - 2])]
            p.emit("tensor", tr, waits=waits, inc=sp.inc())
            def tev(e, d=vtok[:, kc * 128:(kc + 1) * 128], s=pt_ps[slot]):
                return e.tensor_copy(d, s)
            p.emit("vector", tev, waits=[(sp.h, sp.v)], inc=sv.inc())
            trev[kc] = sv.v
        p.flush(f"qkv{c}")

        for hh in range(2):
            hb = 64 * hh
            q_h = q_bf[hb:hb + 64, :]
            k_h = k_bf[hb:hb + 64, :]
            last_exp = sa.v
            last_tp = sp.v
            for qt in range(16):
                lw = qt // 4
                qtp = qt % 4
                w0 = (t0 == 0 and lw == 0)
                if w0:
                    koff, kwid = 0, 512
                    moff = (3 - qtp) * 128 + 512
                elif t0 == 0:
                    koff, kwid = lw * 512 - 512, 1024
                    moff = (3 - qtp) * 128
                else:
                    koff, kwid = lw * 512, 1024
                    moff = (3 - qtp) * 128
                nkt = kwid // 512
                evac_done = {}
                for kt in range(nkt):
                    slot = kt % 2
                    def mm(e, pa=mm_ps[slot], l=q_h[:, qt * 128:(qt + 1) * 128],
                           r=k_h[:, koff + kt * 512: koff + (kt + 1) * 512]):
                        return e.matmul(pa, l, r, start=True, stop=True)
                    p.emit("tensor", mm, waits=[(sv.h, sv.v)], inc=sp.inc())
                    ssl = s_row[:, kt * 512:(kt + 1) * 512]
                    def ev1(e, d=ssl, s=mm_ps[slot],
                            m=mask_sb[:, moff + kt * 512: moff + (kt + 1) * 512]):
                        return e.scalar_tensor_tensor(d, s, 1.0, m, ALU.mult, ALU.add)
                    p.emit("vector", ev1,
                           waits=[(sp.h, sp.v), (sa.h, last_exp)], inc=sv.inc())
                    def ev2(e, d=mp[:, kt:kt + 1], s=ssl):
                        return e.tensor_reduce(d, s, AX, ALU.max)
                    p.emit("vector", ev2, waits=[(sv.h, sv.v)], inc=sv.inc())
                    evac_done[kt] = sv.v
                def ng(e, d=negm[:, :], s=mp[:, 0:nkt]):
                    return e.tensor_reduce(d, s, AX, ALU.max, negate=True)
                p.emit("vector", ng, waits=[(sv.h, sv.v)], inc=sv.inc())
                def ex(e, d=p_bf[:, 0:kwid], s=s_row[:, 0:kwid], b=negm[:, :],
                       a=den[:, :]):
                    return e.activation(d, s, AF.Exp, bias=b, scale=1.0, accum_out=a)
                p.emit("scalar", ex, waits=[(sv.h, sv.v), (sp.h, last_tp)],
                       inc=sa.inc())
                last_exp = sa.v
                def rc(e, d=rden[:, :], s=den[:, :]):
                    return e.reciprocal(d, s)
                p.emit("vector", rc, waits=[(sa.h, sa.v)], inc=sv.inc())
                def nm(e, d=p_bf[:, 0:kwid], s=p_bf[:, 0:kwid], r=rden[:, :]):
                    return e.tensor_scalar(d, s, r, None, ALU.mult)
                p.emit("vector", nm, waits=[(sv.h, sv.v)], inc=sv.inc())
                norm_done = sv.v
                nkc = kwid // 128
                ptev = {}
                for kc in range(nkc):
                    slot = kc % 2
                    def tr(e, pa=pt_ps[slot], s=p_bf[:, kc * 128:(kc + 1) * 128],
                           idt=ident[:, :]):
                        return e.transpose(pa, s, idt)
                    w = [(sv.h, norm_done)]
                    if kc >= 2:
                        w.append((sv.h, ptev[kc - 2]))
                    p.emit("tensor", tr, waits=w, inc=sp.inc())
                    def tev(e, d=pt_sb[slot][:, :], s=pt_ps[slot]):
                        return e.tensor_copy(d, s)
                    p.emit("vector", tev, waits=[(sp.h, sp.v)], inc=sv.inc())
                    ptev[kc] = sv.v
                    def vm(e, pa=out_ps.ap()[:, :],
                           l=vtok[:, (koff + kc * 128) + hb:(koff + kc * 128) + hb + 64],
                           r=pt_sb[slot][:, :], st_=(kc == 0), sp_=(kc == nkc - 1)):
                        return e.matmul(pa, l, r, start=st_, stop=sp_)
                    w = [(sv.h, ptev[kc])]
                    if kc == 0:
                        w.append((sa.h, sa.v))
                    p.emit("tensor", vm, waits=w, inc=sp.inc())
                last_tp = sp.v
                def oev(e, d=o_bf[c][hb:hb + 64, qt * 128:(qt + 1) * 128],
                        s=out_ps.ap()[:, :]):
                    return e.copy(d, s)
                p.emit("scalar", oev, waits=[(sp.h, sp.v)], inc=sa.inc())
            p.flush(f"attn{c}_{hh}")

    for c in range(NCH):
        p.dma("sync", ld, xr[c][:, :], xf[c * 128:(c + 1) * 128, t0:t0 + TOW])
    p.flush("reload_x")
    for oc in range(NCH):
        buf = oc % 2
        load_weight_cols(p, [(sp.h, sp.v), (so.h, so.v)], ld, wo, INNER, oc * 128, 128,
                         w32[buf], wbf[buf], so,
                         extra_cast_waits=[(sp.h, sp.v)])
        for tt in range(4):
            slot = tt % 2
            tsl = slice(tt * 512, (tt + 1) * 512)
            for k in range(4):
                def mm(e, pa=mm_ps[slot], w=wbf[buf][:, k * 128:(k + 1) * 128],
                       r=o_bf[k][:, tsl], st_=(k == 0), sp_=(k == 3)):
                    return e.matmul(pa, w, r, start=st_, stop=sp_)
                waits = [(so.h, so.v), (sv.h, sv.v)] if k == 0 else []
                p.emit("tensor", mm, waits=waits, inc=sp.inc() if k == 3 else None)
            def ev(e, d=xr[oc][:, tsl], s=mm_ps[slot], x=xr[oc][:, tsl]):
                return e.scalar_tensor_tensor(d, s, 1.0, x, ALU.mult, ALU.add)
            p.emit("vector", ev, waits=[(sp.h, sp.v), (ld.h, ld.v)], inc=sv.inc())
        for tt in range(4):
            tsl = slice(tt * 512, (tt + 1) * 512)
            p.dma("sync", ld, xo[oc * 128:(oc + 1) * 128, tsl], xr[oc][:, tsl],
                  waits=[(sv.h, sv.v)])
    p.flush("oproj")
    def fin(e):
        e.wait_ge(ld.h, ld.v)
        return None
    p.emit("sync", fin)
    p.flush("fin")
    st.close()
    return nc


def build_l0f_mlp(t0, do_outer):
    nc = bass.Bass("TRN2", target_bir_lowering=False)
    import contextlib
    st = contextlib.ExitStack()
    NHC = 22
    HP = NHC * 128

    xf = nc.dram_tensor("xf", [DM, SEQ], F32, kind="ExternalInput").ap()
    w1a = nc.dram_tensor("w1a", [DM, HP], F32, kind="ExternalInput").ap()
    w1g = nc.dram_tensor("w1g", [DM, HP], F32, kind="ExternalInput").ap()
    w2 = nc.dram_tensor("w2", [HP, DM], F32, kind="ExternalInput").ap()
    b1ac = nc.dram_tensor("b1ac", [128, NHC], F32, kind="ExternalInput").ap()
    b1gc = nc.dram_tensor("b1gc", [128, NHC], F32, kind="ExternalInput").ap()
    b2c = nc.dram_tensor("b2c", [128, 8], F32, kind="ExternalInput").ap()
    lgb = nc.dram_tensor("lgb", [128, 16], F32, kind="ExternalInput").ap()
    if do_outer:
        xprev = nc.dram_tensor("xprev", [DM, TOW], F32, kind="ExternalInput").ap()
    xo = nc.dram_tensor("xo", [DM, TOW], F32, kind="ExternalOutput").ap()

    t = lambda *a: st.enter_context(nc.sbuf_tensor(*a))
    xhf = [t(f"xhf{c}", [128, TOW + 2], F32) for c in range(NCH)]
    xr = [t(f"xr{c}", [128, TOW], F32) for c in range(NCH)]
    lnh = [xhf[c][:, :].bitcast(BF16)[:, 0:2048] for c in range(NCH)]
    hm = [t(f"hm{c}", [128, 512], BF16) for c in range(NHC)]
    w32 = [t("w32_0", [128, HP], F32)]
    w32 = w32 * 2
    wbf = [t("wbf_0", [128, HP], BF16)]
    wbf = wbf * 2
    b1a_sb = t("b1a_sb", [128, NHC], F32)
    b1g_sb = t("b1g_sb", [128, NHC], F32)
    b2_sb = t("b2_sb", [128, 8], F32)
    lgb_sb = t("lgb_sb", [128, 16], F32)
    tmpa = t("tmpa", [128, 512], F32)
    tmpg = t("tmpg", [128, 512], F32)
    lnx = t("lnx", [128, 512], BF16)
    sq_bf = t("sq_bf", [128, 512], BF16)
    ones_bf = t("ones_bf", [128, 1], BF16)
    ones_row = t("ones_row", [1, 128], F32)
    stat = t("stat", [1, 3 * 512], F32)
    stat2 = t("stat2", [1, 2 * 512], F32)
    out_f = t("out_f", [128, 512], F32)

    ps = st.enter_context(nc.psum_tensor("ps", [128, 2 * 512], F32))
    mm_ps = [ps.ap()[:, 0:512], ps.ap()[:, 512:1024]]
    ps2 = st.enter_context(nc.psum_tensor("ps2", [128, 2 * 512], F32))
    mm2_ps = [ps2.ap()[:, 0:512], ps2.ap()[:, 512:1024]]
    st_ps = st.enter_context(nc.psum_tensor("st_ps", [1, 512], F32))
    st2_ps = st.enter_context(nc.psum_tensor("st2_ps", [1, 512], F32))
    bc_ps = mm_ps[0]
    bc2_ps = mm_ps[1]

    ld = Sem(st, nc, "ld")
    sp = Sem(st, nc, "sp")
    sv = Sem(st, nc, "sv")
    sa = Sem(st, nc, "sa")
    so = Sem(st, nc, "so")

    p = Prog(nc)

    p.dma("sync", ld, b1a_sb[:, :], b1ac)
    p.dma("sync", ld, b1g_sb[:, :], b1gc)
    p.dma("sync", ld, b2_sb[:, :], b2c)
    p.dma("sync", ld, lgb_sb[:, :], lgb)
    for c in range(NCH):
        if t0 == 0:
            def zc(e, d=xhf[c][:, 0:1]):
                return e.memset(d, 0.0)
            p.emit("vector", zc, inc=sv.inc())
            p.dma("sync", ld, xhf[c][:, 1:TOW + 1],
                  xf[c * 128:(c + 1) * 128, 0:TOW], waits=[(sv.h, sv.v)])
        else:
            p.dma("sync", ld, xhf[c][:, 0:TOW + 1],
                  xf[c * 128:(c + 1) * 128, t0 - 1:t0 + TOW])
        p.dma("sync", ld, xr[c][:, :], xf[c * 128:(c + 1) * 128, t0:t0 + TOW])
    def ones_e(e, d=ones_row[0:1, :]):
        return e.memset(d, 1.0)
    p.emit("vector", ones_e, inc=sv.inc())
    def ones_e2(e, d=ones_bf[:, :]):
        return e.memset(d, 1.0)
    p.emit("vector", ones_e2, inc=sv.inc())
    p.flush("load")

    def hsl(c, tsl):
        off = 1 if c < 4 else 0
        return xhf[c][:, off + tsl.start: off + tsl.stop]

    gcol = [lgb_sb[:, c:c + 1] for c in range(8)]
    bcol = [lgb_sb[:, 8 + c:9 + c] for c in range(8)]
    emit_ln(p, nc, None, gcol, bcol, TOW, lnx, sq_bf, (ones_bf, ones_row), stat,
            stat2, st_ps, st2_ps, bc_ps, bc2_ps, out_f, None, ld, sp, sv, sa, so,
            1e-5, out_bf=lnh, dma_out=False, in_slices=hsl)
    p.flush("ln_inner")

    for tt in range(4):
        tsl = slice(tt * 512, (tt + 1) * 512)
        for oc in range(NHC):
            for role in range(2):
                wsrc = w1a if role == 0 else w1g
                buf = role
                load_weight_cols(p, [(sp.h, sp.v), (so.h, so.v)], ld, wsrc, DM,
                                 oc * 128, 128, w32[buf], wbf[buf], so,
                                 extra_cast_waits=[(sp.h, sp.v)])
                dst = (mm_ps if role == 0 else mm2_ps)[oc % 2]
                for k in range(NCH):
                    def mm(e, pa=dst, w=wbf[buf][:, k * 128:(k + 1) * 128],
                           r=lnh[k][:, tsl], st_=(k == 0), sp_=(k == NCH - 1)):
                        return e.matmul(pa, w, r, start=st_, stop=sp_)
                    waits = [(so.h, so.v), (sv.h, sv.v), (sa.h, sa.v)] if k == 0 else []
                    p.emit("tensor", mm, waits=waits,
                           inc=sp.inc() if k == NCH - 1 else None)
            def e1(e, d=tmpa[:, :], s=mm_ps[oc % 2], b=b1a_sb[:, oc:oc + 1]):
                return e.tensor_scalar(d, s, b, None, ALU.add)
            p.emit("vector", e1, waits=[(sp.h, sp.v)], inc=sv.inc())
            def e2(e, d=tmpg[:, :], s=mm2_ps[oc % 2], b=b1g_sb[:, oc:oc + 1]):
                return e.activation(d, s, AF.Gelu, bias=b)
            p.emit("scalar", e2, waits=[(sp.h, sp.v)], inc=sa.inc())
            def e3(e, d=hm[oc][:, :], a=tmpa[:, :], b=tmpg[:, :]):
                return e.tensor_mul(d, a, b)
            p.emit("vector", e3, waits=[(sv.h, sv.v), (sa.h, sa.v)], inc=sv.inc())
        for oc in range(NCH):
            load_weight_cols(p, [(sp.h, sp.v), (so.h, so.v)], ld, w2, HP,
                             oc * 128, 128, w32[0], wbf[0], so,
                             extra_cast_waits=[(sp.h, sp.v)])
            for k in range(NHC):
                def mm(e, pa=mm_ps[oc % 2], w=wbf[0][:, k * 128:(k + 1) * 128],
                       r=hm[k][:, :], st_=(k == 0), sp_=(k == NHC - 1)):
                    return e.matmul(pa, w, r, start=st_, stop=sp_)
                waits = [(so.h, so.v), (sv.h, sv.v)] if k == 0 else []
                p.emit("tensor", mm, waits=waits,
                       inc=sp.inc() if k == NHC - 1 else None)
            def e4(e, d=xr[oc][:, tsl], s=mm_ps[oc % 2],
                   b=b2_sb[:, oc:oc + 1], x=xr[oc][:, tsl]):
                return e.scalar_tensor_tensor(d, s, b, x, ALU.add, ALU.add)
            p.emit("vector", e4, waits=[(sp.h, sp.v), (ld.h, ld.v)], inc=sv.inc())
        p.flush(f"mlp_{tt}")

    if do_outer:
        for c in range(NCH):
            p.dma("sync", ld, xhf[c][:, 0:TOW],
                  xprev[c * 128:(c + 1) * 128, :])
            def e5(e, d=xr[c][:, :], a=xr[c][:, :], b=xhf[c][:, 0:TOW]):
                return e.tensor_add(d, a, b)
            p.emit("vector", e5, waits=[(ld.h, ld.v), (sv.h, sv.v)], inc=sv.inc())
        p.flush("outer_res")
        emit_ln(p, nc, xr, None, None, TOW, lnx, sq_bf, (ones_bf, ones_row), stat,
                stat2, st_ps, st2_ps, bc_ps, bc2_ps, out_f, xo, ld, sp, sv, sa, so,
                1e-5)
        p.flush("ln_outer")
    else:
        for c in range(NCH):
            for tt in range(4):
                tsl = slice(tt * 512, (tt + 1) * 512)
                p.dma("sync", ld, xo[c * 128:(c + 1) * 128, tsl], xr[c][:, tsl],
                      waits=[(sv.h, sv.v)])
        p.flush("store")
    def fin(e):
        e.wait_ge(ld.h, ld.v)
        return None
    p.emit("sync", fin)
    p.flush("fin")
    st.close()
    return nc


def _l0f_tables(t0, l, ins):
    kstart = max(0, t0 - 512)
    klen = t0 + TOW - kstart
    DIM_HEAD, WINDOW = 64, 512
    half_idx = np.arange(0, DIM_HEAD, 2, dtype=np.float32)
    inv_freq = 1.0 / 10000.0 ** (half_idx / DIM_HEAD)
    def tabs(tstart, tlen, is_q):
        tpos = np.arange(tstart, tstart + tlen, dtype=np.float32)
        f_half = tpos[:, None] * inv_freq[None, :]
        freqs = np.concatenate([f_half, f_half], axis=-1)
        base = (half_idx + 0.4 * DIM_HEAD) / (1.4 * DIM_HEAD)
        power = (tpos - SEQ // 2) / float(WINDOW // 2)
        scale = base[None, :] ** power[:, None]
        scale = np.concatenate([scale, scale], axis=-1)
        if not is_q:
            scale = 1.0 / scale
        cs = (np.cos(freqs) * scale).astype(np.float32)
        ss = (np.sin(freqs) * scale).astype(np.float32)
        sgn = np.ones(64, np.float32)
        sgn[0:32] = -1.0
        ssg = ss * sgn[None, :]
        csf = np.concatenate([cs.T, cs.T], axis=0)
        ssf = np.concatenate([ssg.T, ssg.T], axis=0)
        return np.ascontiguousarray(
            np.concatenate([csf, ssf], axis=1).astype(np.float32))
    tabq = tabs(t0, TOW, True)
    tabk = tabs(kstart, klen, False)
    qs = ins["L0_ffn_q_scale"][l].astype(np.float32)
    ks = ins["L0_ffn_k_scale"][l].astype(np.float32)
    qks = np.stack([np.concatenate([qs, qs]), np.concatenate([ks, ks])], axis=1)
    bd2 = np.zeros((128, 2), BF)
    bd2[0:64, 0] = 1
    bd2[64:128, 1] = 1
    bd2t = np.zeros((2, 128), np.float32)
    bd2t[0, 0:64] = 1
    bd2t[1, 64:128] = 1
    il = np.arange(128)[:, None]
    cc = np.arange(1408)[None, :]
    delta = cc - 384 - il
    maskt = np.where((delta >= 0) & (delta <= 512), 0.0, NEG).astype(np.float32)
    return tabq, tabk, np.ascontiguousarray(qks.astype(np.float32)), bd2, bd2t, maskt


def run_l0_ffn(x_fm, ins):
    x_l0a = x_fm
    x = x_fm
    for l in range(2):
        out = np.empty((NB, DM, SEQ), np.float32)
        for half in range(2):
            t0 = half * TOW
            tabq, tabk, qks, bd2, bd2t, maskt = _l0f_tables(t0, l, ins)
            nc = build_l0f_attn(t0)
            extras = {
                "wqkv": ins["L0_ffn_wqkv"][l].astype(np.float32),
                "wo": ins["L0_ffn_wo"][l].astype(np.float32),
                "qks": qks, "tabq": tabq, "tabk": tabk,
                "bd2": bd2, "bd2t": bd2t, "maskt": maskt,
                "ident_in": np.eye(128, dtype=BF),
            }
            in_maps = []
            cores = []
            for b in range(NB):
                cores.append(2 * b + half)
                im = {"xf": np.ascontiguousarray(x[b])}
                im.update(extras)
                in_maps.append(im)
            res = run_bass_kernel_spmd(nc, in_maps, core_ids=cores)
            for i, b in enumerate(range(NB)):
                out[b, :, t0:t0 + TOW] = res.results[i]["xo"]
        x = out
        w1 = ins["L0_ffn_w1"][l].astype(np.float32)
        w1a = np.zeros((DM, 2816), np.float32)
        w1a[:, 0:2730] = w1[:, 0:2730]
        w1g = np.zeros((DM, 2816), np.float32)
        w1g[:, 0:2730] = w1[:, 2730:5460]
        w2 = np.zeros((2816, DM), np.float32)
        w2[0:2730, :] = ins["L0_ffn_w2"][l].astype(np.float32)
        b1 = ins["L0_ffn_b1"][l].astype(np.float32)
        b1a = np.zeros(2816, np.float32); b1a[0:2730] = b1[0:2730]
        b1g = np.zeros(2816, np.float32); b1g[0:2730] = b1[2730:5460]
        lgb = np.concatenate([vec_chunks(ins["L0_ffn_ln_g"][l]),
                              vec_chunks(ins["L0_ffn_ln_b"][l])], axis=1)
        do_outer = (l == 1)
        out = np.empty((NB, DM, SEQ), np.float32)
        for half in range(2):
            t0 = half * TOW
            nc = build_l0f_mlp(t0, do_outer)
            extras = {
                "w1a": w1a, "w1g": w1g, "w2": w2,
                "b1ac": vec_chunks(b1a), "b1gc": vec_chunks(b1g),
                "b2c": vec_chunks(ins["L0_ffn_b2"][l]),
                "lgb": lgb,
            }
            in_maps = []
            cores = []
            for b in range(NB):
                cores.append(2 * b + half)
                im = {"xf": np.ascontiguousarray(x[b])}
                if do_outer:
                    im["xprev"] = np.ascontiguousarray(x_l0a[b, :, t0:t0 + TOW])
                im.update(extras)
                in_maps.append(im)
            res = run_bass_kernel_spmd(nc, in_maps, core_ids=cores)
            for i, b in enumerate(range(NB)):
                out[b, :, t0:t0 + TOW] = res.results[i]["xo"]
        x = out
    return x



def build_l3_attn(t0, stage=9):
    nc = bass.Bass("TRN2", target_bir_lowering=False)
    import contextlib
    st = contextlib.ExitStack()

    xf = nc.dram_tensor("xf", [DM, SEQ], F32, kind="ExternalInput").ap()
    wq = nc.dram_tensor("wq", [DM, DM], F32, kind="ExternalInput").ap()
    wk = nc.dram_tensor("wk", [DM, DM], F32, kind="ExternalInput").ap()
    wv = nc.dram_tensor("wv", [DM, DM], F32, kind="ExternalInput").ap()
    relk2 = nc.dram_tensor("relk2", [128, 33], BF16, kind="ExternalInput").ap()
    relvt = nc.dram_tensor("relvt", [33, 64], BF16, kind="ExternalInput").ap()
    wpr = nc.dram_tensor("wpr", [DM, DM], F32, kind="ExternalInput").ap()
    bpr = nc.dram_tensor("bpr", [128, 8], F32, kind="ExternalInput").ap()
    bmasks = nc.dram_tensor("bmasks", [128, 4 * 384], F32, kind="ExternalInput").ap()
    ident_in = nc.dram_tensor("ident_in", [128, 128], BF16, kind="ExternalInput").ap()
    xo = nc.dram_tensor("xo", [DM, TOW], F32, kind="ExternalOutput").ap()

    dreg = nc.dram_tensor("dreg", [128 * 384 + 64], F32).ap()
    dreg2 = nc.dram_tensor("dreg2", [128 * 384 + 64], BF16).ap()

    t = lambda *a: st.enter_context(nc.sbuf_tensor(*a))
    xr = [t(f"xr{c}", [128, TOW], F32) for c in range(NCH)]
    xbf = [xr[c][:, :].bitcast(BF16) for c in range(NCH)]
    q_bf = t("q_bf", [128, TOW], BF16)
    k_bf = t("k_bf", [128, SEQ], BF16)
    v_fm = t("v_fm", [128, SEQ], BF16)
    vtok = t("vtok", [128, SEQ], BF16)
    o_bf = [t(f"o{c}", [128, TOW], BF16) for c in range(NCH)]
    s_row = t("s_row", [128, SEQ], F32)
    p_bf = t("p_bf", [128, SEQ], BF16)
    relk_sb = t("relk_sb", [128, 33], BF16)
    relv_sb = t("relv_sb", [33, 64], BF16)
    qrel_sb = t("qrel_sb", [128, 33], F32)
    band_sb = t("band_sb", [128, 384], F32)
    preg_sb = t("preg_sb", [128, 384], BF16)
    bandp_sb = t("bandp_sb", [128, 33], F32)
    mask_sb = t("mask_sb", [128, 4 * 384], F32)
    w_sb = t("w_sb", [128, 36], F32)
    w_bfs = t("w_bfs", [128, 33], BF16)
    wt_sb = t("wt_sb", [33, 128], BF16)
    cs = t("cs", [128, 32], F32)
    tmp384 = t("tmp384", [128, 384], F32)
    ident = t("ident", [128, 128], BF16)
    w32 = [t(f"w32_{i}", [128, 1024], F32) for i in range(2)]
    wbf = [t(f"wbf_{i}", [128, 1024], BF16) for i in range(2)]
    bpr_sb = t("bpr_sb", [128, 8], F32)
    pt_sb = [t(f"pt{i}", [128, 128], BF16) for i in range(2)]
    mp = t("mp", [128, 32], F32)
    negm = t("negm", [128, 1], F32)
    den = t("den", [128, 1], F32)
    rden = t("rden", [128, 1], F32)
    sq_bf = t("sq_bf", [128, 512], BF16)
    ones_bf = t("ones_bf", [128, 1], BF16)
    ones_row = t("ones_row", [1, 128], F32)
    stat = t("stat", [1, 3 * 512], F32)
    stat2 = t("stat2", [1, 2 * 512], F32)
    out_f = t("out_f", [128, 512], F32)

    ps = st.enter_context(nc.psum_tensor("ps", [128, 2 * 512], F32))
    mm_ps = [ps.ap()[:, 0:512], ps.ap()[:, 512:1024]]
    s_ps = [ps.ap()[:, 0:128], ps.ap()[:, 512:640]]
    pt_ps_t = st.enter_context(nc.psum_tensor("pt_ps", [128, 2 * 1024], BF16))
    pt_ps = [pt_ps_t.ap()[:, 0:128], pt_ps_t.ap()[:, 1024:1152]]
    out_ps = st.enter_context(nc.psum_tensor("out_ps", [64, 128], F32))
    wt_ps_t = st.enter_context(nc.psum_tensor("wt_ps", [33, 1024], BF16))
    wt_ps = wt_ps_t.ap()[:, 0:128]
    st_ps = st.enter_context(nc.psum_tensor("st_ps", [1, 512], F32))
    st2_ps = st.enter_context(nc.psum_tensor("st2_ps", [1, 512], F32))
    bc_ps = mm_ps[0]
    bc2_ps = mm_ps[1]

    ld = Sem(st, nc, "ld")
    sp = Sem(st, nc, "sp")
    sv = Sem(st, nc, "sv")
    sa = Sem(st, nc, "sa")
    so = Sem(st, nc, "so")
    sx = Sem(st, nc, "sx")

    p = Prog(nc)

    p.dma("sync", ld, ident[:, :], ident_in)
    p.dma("sync", ld, relk_sb[:, :], relk2)
    p.dma("sync", ld, relv_sb[:, :], relvt)
    p.dma("sync", ld, bpr_sb[:, :], bpr)
    p.dma("sync", ld, mask_sb[:, :], bmasks)
    def ones_e(e, d=ones_row[0:1, :]):
        return e.memset(d, 1.0)
    p.emit("vector", ones_e, inc=sv.inc())
    def z1(e, d=tmp384[:, :]):
        return e.memset(d, 0.0)
    p.emit("vector", z1, inc=sv.inc())
    p.emit("gpsimd",
           lambda e, d=AP(dreg.tensor, 0, [[384, 128], [1, 384]]), s=tmp384[:, :]:
           e.dma_start(out=d, in_=s), waits=[(sv.h, sv.v)], inc=sx.inc(16))
    for c in range(NCH):
        for hh in range(2):
            dst = s_row[:, hh * 2048:(hh + 1) * 2048]
            srcd = xf[c * 128:(c + 1) * 128, hh * 2048:(hh + 1) * 2048]
            waits = [(sv.h, sv.v)] if (c + hh) else []
            p.dma("sync", ld, dst, srcd, waits=waits)
            def cast(e, d=xbf[c][:, hh * 2048:(hh + 1) * 2048], s=dst):
                return e.tensor_copy(d, s)
            p.emit("vector", cast, waits=[(ld.h, ld.v)], inc=sv.inc())
    p.flush("load_x")

    ML = mask_sb[:, 0:384]
    MR = mask_sb[:, 384:768]
    MWL = mask_sb[:, 768:1152]
    MWR = mask_sb[:, 1152:1536]

    for c in range(NCH):
        specs = [("q", wq, c * 128), ("k", wk, c * 128), ("v", wv, c * 128)]
        for i, (role, wsrc, c0) in enumerate(specs):
            buf = i % 2
            load_weight_cols(p, [(sp.h, sp.v), (so.h, so.v)], ld, wsrc, DM, c0, 128,
                             w32[buf], wbf[buf], so,
                             extra_cast_waits=[(sp.h, sp.v)])
            ntt = 4 if role == "q" else 8
            toff = t0 if role == "q" else 0
            for tt in range(ntt):
                slot = tt % 2
                tsl = slice(toff + tt * 512, toff + (tt + 1) * 512)
                lsl = slice(tt * 512, (tt + 1) * 512)
                for k in range(NCH):
                    def mm(e, pa=mm_ps[slot], w=wbf[buf][:, k * 128:(k + 1) * 128],
                           r=xbf[k][:, tsl], st_=(k == 0), sp_=(k == NCH - 1)):
                        return e.matmul(pa, w, r, start=st_, stop=sp_)
                    waits = [(so.h, so.v), (sv.h, sv.v)] if k == 0 else []
                    p.emit("tensor", mm, waits=waits,
                           inc=sp.inc() if k == NCH - 1 else None)
                if role == "q":
                    def ev(e, d=q_bf[:, lsl], s=mm_ps[slot]):
                        return e.tensor_scalar(d, s, 0.125, None, ALU.mult)
                elif role == "k":
                    def ev(e, d=k_bf[:, lsl], s=mm_ps[slot]):
                        return e.tensor_copy(d, s)
                else:
                    def ev(e, d=v_fm[:, lsl], s=mm_ps[slot]):
                        return e.tensor_copy(d, s)
                p.emit("vector", ev, waits=[(sp.h, sp.v)], inc=sv.inc())
        trev = {}
        for kc in range(SEQ // 128):
            slot = kc % 2
            def tr(e, pa=pt_ps[slot], s=v_fm[:, kc * 128:(kc + 1) * 128], idt=ident[:, :]):
                return e.transpose(pa, s, idt)
            waits = [(sv.h, sv.v)] if kc < 2 else [(sv.h, trev[kc - 2])]
            p.emit("tensor", tr, waits=waits, inc=sp.inc())
            def tev(e, d=vtok[:, kc * 128:(kc + 1) * 128], s=pt_ps[slot]):
                return e.tensor_copy(d, s)
            p.emit("vector", tev, waits=[(sp.h, sp.v)], inc=sv.inc())
            trev[kc] = sv.v
        p.flush(f"kv{c}")

        for hh in range(2):
            hb = 64 * hh
            q_h = q_bf[hb:hb + 64, :]
            k_h = k_bf[hb:hb + 64, :]
            last_exp = sa.v
            last_tp = sp.v
            for qt in range(16):
                q0 = t0 + qt * 128
                qtk = q0 // 128
                near = [n for n in (qtk - 1, qtk, qtk + 1) if 0 <= n < 32]
                def qm(e, pa=mm_ps[0][:, 0:33], l=q_h[:, qt * 128:(qt + 1) * 128],
                       r=relk_sb[hb:hb + 64, :]):
                    return e.matmul(pa, l, r, start=True, stop=True)
                p.emit("tensor", qm, waits=[(sv.h, sv.v), (sa.h, last_exp)],
                       inc=sp.inc())
                def qev(e, d=qrel_sb[:, :], s=mm_ps[0][:, 0:33]):
                    return e.tensor_copy(d, s)
                p.emit("vector", qev, waits=[(sp.h, sp.v)], inc=sv.inc())
                def qwd(e, d=AP(dreg.tensor, 112, [[385, 128], [1, 33]]),
                        s=qrel_sb[:, :]):
                    return e.dma_start(out=d, in_=s)
                p.emit("gpsimd", qwd, waits=[(sv.h, sv.v), (sx.h, sx.v)],
                       inc=sx.inc(16))
                def qrd(e, d=band_sb[:, :],
                        s=AP(dreg.tensor, 0, [[384, 128], [1, 384]])):
                    return e.dma_start(out=d, in_=s)
                p.emit("gpsimd", qrd, waits=[(sx.h, sx.v)], inc=sx.inc(16))
                def bb1(e, d=band_sb[:, :], s=ML, sc=qrel_sb[:, 0:1], b=band_sb[:, :]):
                    return e.scalar_tensor_tensor(d, s, sc, b, ALU.mult, ALU.add)
                p.emit("vector", bb1, waits=[(sx.h, sx.v)], inc=sv.inc())
                def bb2(e, d=band_sb[:, :], s=MR, sc=qrel_sb[:, 32:33], b=band_sb[:, :]):
                    return e.scalar_tensor_tensor(d, s, sc, b, ALU.mult, ALU.add)
                p.emit("vector", bb2, waits=[(sv.h, sv.v)], inc=sv.inc())
                evac_done = {}
                for kt in range(32):
                    slot = kt % 2
                    def mm(e, pa=s_ps[slot], l=q_h[:, qt * 128:(qt + 1) * 128],
                           r=k_h[:, kt * 128:(kt + 1) * 128]):
                        return e.matmul(pa, l, r, start=True, stop=True)
                    waits = [(sv.h, evac_done[kt - 2])] if kt >= 2 else [(sv.h, sv.v)]
                    p.emit("tensor", mm, waits=waits, inc=sp.inc())
                    ssl = s_row[:, kt * 128:(kt + 1) * 128]
                    if kt in near:
                        boff = (kt - (qtk - 1)) * 128
                        def ev1(e, d=ssl, s=s_ps[slot],
                                b=band_sb[:, boff:boff + 128]):
                            return e.tensor_add(d, s, b)
                        p.emit("vector", ev1,
                               waits=[(sp.h, sp.v), (sa.h, last_exp)], inc=sv.inc())
                        def ev2(e, d=mp[:, kt:kt + 1], s=ssl):
                            return e.tensor_reduce(d, s, AX, ALU.max)
                        p.emit("vector", ev2, waits=[(sv.h, sv.v)], inc=sv.inc())
                    else:
                        qcol = qrel_sb[:, 0:1] if kt < qtk else qrel_sb[:, 32:33]
                        def ev1(e, d=ssl, s=s_ps[slot], b=qcol,
                                a=mp[:, kt:kt + 1]):
                            return e.tensor_scalar(d, s, b, None, ALU.add,
                                                   ALU.max, accum_out=a)
                        p.emit("vector", ev1,
                               waits=[(sp.h, sp.v), (sa.h, last_exp)], inc=sv.inc())
                    evac_done[kt] = sv.v
                def ng(e, d=negm[:, :], s=mp[:, :]):
                    return e.tensor_reduce(d, s, AX, ALU.max, negate=True)
                p.emit("vector", ng, waits=[(sv.h, sv.v)], inc=sv.inc())
                for kt in range(32):
                    def ex(e, d=p_bf[:, kt * 128:(kt + 1) * 128],
                           s=s_row[:, kt * 128:(kt + 1) * 128], b=negm[:, :],
                           a=cs[:, kt:kt + 1]):
                        return e.activation(d, s, AF.Exp, bias=b, scale=1.0,
                                            accum_out=a)
                    p.emit("scalar", ex,
                           waits=[(sv.h, sv.v), (sp.h, last_tp), (sa.h, sa.v)],
                           inc=sa.inc())
                last_exp = sa.v
                def dn(e, d=den[:, :], s=cs[:, :]):
                    return e.tensor_reduce(d, s, AX, ALU.add)
                p.emit("vector", dn, waits=[(sa.h, sa.v)], inc=sv.inc())
                def rc(e, d=rden[:, :], s=den[:, :]):
                    return e.reciprocal(d, s)
                p.emit("vector", rc, waits=[(sv.h, sv.v)], inc=sv.inc())
                r0 = qtk - 1
                def pz(e, d=preg_sb[:, :]):
                    return e.memset(d, 0.0)
                p.emit("vector", pz, waits=[(sx.h, sx.v)], inc=sv.inc())
                lo = max(0, r0)
                hi = min(32, r0 + 3)
                def pcp(e, d=preg_sb[:, (lo - r0) * 128:(hi - r0) * 128],
                        s=p_bf[:, lo * 128:hi * 128]):
                    return e.tensor_copy(d, s)
                p.emit("vector", pcp, waits=[(sv.h, sv.v)], inc=sv.inc())
                def pwd(e, d=AP(dreg2.tensor, 0, [[384, 128], [1, 384]]),
                        s=preg_sb[:, :]):
                    return e.dma_start(out=d, in_=s)
                p.emit("gpsimd", pwd, waits=[(sv.h, sv.v), (sx.h, sx.v)],
                       inc=sx.inc(16))
                def prd(e, d=bandp_sb[:, :],
                        s=AP(dreg2.tensor, 112, [[385, 128], [1, 33]])):
                    return e.dma_start(out=d, in_=s)
                p.emit("gpsimd", prd, waits=[(sx.h, sx.v)], inc=sx.inc(16))
                def q1(e, d=tmp384[:, :], a=preg_sb[:, :], b=MWL):
                    return e.tensor_mul(d, a, b)
                p.emit("vector", q1, waits=[(sx.h, sx.v)], inc=sv.inc())
                def q2(e, d=w_sb[:, 33:34], s=tmp384[:, :]):
                    return e.tensor_reduce(d, s, AX, ALU.add)
                p.emit("vector", q2, waits=[(sv.h, sv.v)], inc=sv.inc())
                def q3(e, d=tmp384[:, :], a=preg_sb[:, :], b=MWR):
                    return e.tensor_mul(d, a, b)
                p.emit("vector", q3, waits=[(sv.h, sv.v)], inc=sv.inc())
                def q4(e, d=w_sb[:, 34:35], s=tmp384[:, :]):
                    return e.tensor_reduce(d, s, AX, ALU.add)
                p.emit("vector", q4, waits=[(sv.h, sv.v)], inc=sv.inc())
                if r0 > 0:
                    def q5(e, d=w_sb[:, 35:36], s=cs[:, 0:r0]):
                        return e.tensor_reduce(d, s, AX, ALU.add)
                    p.emit("vector", q5, waits=[(sv.h, sv.v)], inc=sv.inc())
                else:
                    def q5(e, d=w_sb[:, 35:36]):
                        return e.memset(d, 0.0)
                    p.emit("vector", q5, waits=[(sv.h, sv.v)], inc=sv.inc())
                def a1(e, d=w_sb[:, 0:33], s=bandp_sb[:, :]):
                    return e.tensor_copy(d, s)
                p.emit("vector", a1, waits=[(sx.h, sx.v), (sv.h, sv.v)], inc=sv.inc())
                def a2(e, d=w_sb[:, 0:1], a=w_sb[:, 33:34], b=w_sb[:, 35:36]):
                    return e.tensor_add(d, a, b)
                p.emit("vector", a2, waits=[(sv.h, sv.v)], inc=sv.inc())
                if r0 + 3 < 32:
                    def q6(e, d=w_sb[:, 35:36], s=cs[:, r0 + 3:32]):
                        return e.tensor_reduce(d, s, AX, ALU.add)
                    p.emit("vector", q6, waits=[(sv.h, sv.v)], inc=sv.inc())
                else:
                    def q6(e, d=w_sb[:, 35:36]):
                        return e.memset(d, 0.0)
                    p.emit("vector", q6, waits=[(sv.h, sv.v)], inc=sv.inc())
                def a3(e, d=w_sb[:, 32:33], a=w_sb[:, 34:35], b=w_sb[:, 35:36]):
                    return e.tensor_add(d, a, b)
                p.emit("vector", a3, waits=[(sv.h, sv.v)], inc=sv.inc())
                def a4(e, d=w_bfs[:, :], s=w_sb[:, 0:33], m=rden[:, :]):
                    return e.tensor_scalar(d, s, m, None, ALU.mult)
                p.emit("vector", a4, waits=[(sv.h, sv.v)], inc=sv.inc())
                def wtr(e, pa=wt_ps, s=w_bfs[:, :], idt=ident[:, :]):
                    return e.transpose(pa, s, idt)
                p.emit("tensor", wtr, waits=[(sv.h, sv.v)], inc=sp.inc())
                def wte(e, d=wt_sb[:, :], s=wt_ps):
                    return e.tensor_copy(d, s)
                p.emit("vector", wte, waits=[(sp.h, sp.v)], inc=sv.inc())
                wt_done = sv.v
                def nm(e, d=p_bf[:, :], s=p_bf[:, :], r=rden[:, :]):
                    return e.tensor_scalar(d, s, r, None, ALU.mult)
                p.emit("vector", nm, waits=[(sv.h, sv.v)], inc=sv.inc())
                norm_done = sv.v
                ptev = {}
                for kc in range(32):
                    slot = kc % 2
                    def tr(e, pa=pt_ps[slot], s=p_bf[:, kc * 128:(kc + 1) * 128],
                           idt=ident[:, :]):
                        return e.transpose(pa, s, idt)
                    w = [(sv.h, norm_done)]
                    if kc >= 2:
                        w.append((sv.h, ptev[kc - 2]))
                    p.emit("tensor", tr, waits=w, inc=sp.inc())
                    def tev(e, d=pt_sb[slot][:, :], s=pt_ps[slot]):
                        return e.tensor_copy(d, s)
                    p.emit("vector", tev, waits=[(sp.h, sp.v)], inc=sv.inc())
                    ptev[kc] = sv.v
                    def vm(e, pa=out_ps.ap()[:, :],
                           l=vtok[:, kc * 128 + hb:kc * 128 + hb + 64],
                           r=pt_sb[slot][:, :], st_=(kc == 0)):
                        return e.matmul(pa, l, r, start=st_, stop=False)
                    w = [(sv.h, ptev[kc])]
                    if kc == 0:
                        w.append((sa.h, sa.v))
                    p.emit("tensor", vm, waits=w, inc=sp.inc())
                def vm2(e, pa=out_ps.ap()[:, :], l=relv_sb[:, :], r=wt_sb[:, :]):
                    return e.matmul(pa, l, r, start=False, stop=True)
                p.emit("tensor", vm2, waits=[(sv.h, wt_done)], inc=sp.inc())
                last_tp = sp.v
                def oev(e, d=o_bf[c][hb:hb + 64, qt * 128:(qt + 1) * 128],
                        s=out_ps.ap()[:, :]):
                    return e.copy(d, s)
                p.emit("scalar", oev, waits=[(sp.h, sp.v)], inc=sa.inc())
            p.flush(f"attn{c}_{hh}")

    for c in range(NCH):
        p.dma("sync", ld, xr[c][:, :], xf[c * 128:(c + 1) * 128, t0:t0 + TOW])
    p.flush("reload_x")
    for oc in range(NCH):
        buf = oc % 2
        load_weight_cols(p, [(sp.h, sp.v), (so.h, so.v)], ld, wpr, DM, oc * 128, 128,
                         w32[buf], wbf[buf], so,
                         extra_cast_waits=[(sp.h, sp.v)])
        for tt in range(4):
            slot = tt % 2
            tsl = slice(tt * 512, (tt + 1) * 512)
            for k in range(NCH):
                def mm(e, pa=mm_ps[slot], w=wbf[buf][:, k * 128:(k + 1) * 128],
                       r=o_bf[k][:, tsl], st_=(k == 0), sp_=(k == NCH - 1)):
                    return e.matmul(pa, w, r, start=st_, stop=sp_)
                waits = [(so.h, so.v), (sv.h, sv.v)] if k == 0 else []
                p.emit("tensor", mm, waits=waits, inc=sp.inc() if k == NCH - 1 else None)
            def ev(e, d=xr[oc][:, tsl], s=mm_ps[slot],
                   b=bpr_sb[:, oc:oc + 1], x=xr[oc][:, tsl]):
                return e.scalar_tensor_tensor(d, s, b, x, ALU.add, ALU.add)
            p.emit("vector", ev, waits=[(sp.h, sp.v)], inc=sv.inc())
    p.flush("oproj")
    def ones_set(e, d=ones_bf[:, :]):
        return e.memset(d, 1.0)
    p.emit("vector", ones_set, inc=sv.inc())
    emit_ln(p, nc, xr, None, None, TOW, p_bf, sq_bf, (ones_bf, ones_row), stat,
            stat2, st_ps, st2_ps, bc_ps, bc2_ps, out_f, xo, ld, sp, sv, sa, so,
            1e-5)
    p.flush("ln")
    def fin(e):
        e.wait_ge(ld.h, ld.v)
        return None
    p.emit("sync", fin)
    p.flush("fin")
    st.close()
    return nc


def run_l3_attn(x_fm, ins):
    rk = ins["L3_attn_rel_k_table"].astype(np.float32)
    relk2 = np.concatenate([rk.T, rk.T], axis=0).astype(BF)
    relvt = ins["L3_attn_rel_v_table"].astype(BF)
    il = np.arange(128)[:, None]
    jj = np.arange(384)[None, :]
    delta = jj - il - 128
    ML = (delta < -16).astype(np.float32)
    MR = (delta > 16).astype(np.float32)
    MWL = (delta <= -16).astype(np.float32)
    MWR = (delta >= 16).astype(np.float32)
    bmasks = np.concatenate([ML, MR, MWL, MWR], axis=1)
    extras = {
        "wq": np.ascontiguousarray(ins["L3_attn_Wq"].T).astype(np.float32),
        "wk": np.ascontiguousarray(ins["L3_attn_Wk"].T).astype(np.float32),
        "wv": np.ascontiguousarray(ins["L3_attn_Wv"].T).astype(np.float32),
        "relk2": np.ascontiguousarray(relk2), "relvt": np.ascontiguousarray(relvt),
        "wpr": np.ascontiguousarray(ins["L3_attn_Wo"].T).astype(np.float32),
        "bpr": vec_chunks(ins["L3_attn_bo"]),
        "bmasks": np.ascontiguousarray(bmasks.astype(np.float32)),
        "ident_in": np.eye(128, dtype=BF),
    }
    out = np.empty((NB, DM, SEQ), np.float32)
    for half in range(2):
        t0 = half * TOW
        nc = build_l3_attn(t0)
        in_maps = []
        cores = []
        for b in range(NB):
            cores.append(2 * b + half)
            im = {"xf": np.ascontiguousarray(x_fm[b])}
            im.update(extras)
            in_maps.append(im)
        res = run_bass_kernel_spmd(nc, in_maps, core_ids=cores)
        for i, b in enumerate(range(NB)):
            out[b, :, t0:t0 + TOW] = res.results[i]["xo"]
    return out


def run_causal_mha(x_fm, kind, ins):
    masks4 = np.zeros((128, 4 * 512), np.float32)
    for mv in range(4):
        il = np.arange(128)[:, None]
        jl = np.arange(512)[None, :]
        masks4[:, mv * 512:(mv + 1) * 512] = np.where(jl <= il + mv * 128, 0.0, NEG)
    ident = np.eye(128, dtype=BF)
    if kind == "l0":
        extras = {
            "wat": ins["L0_attn_W_attn"].astype(np.float32),
            "bat": vec_chunks(ins["L0_attn_b_attn"]),
            "wpr": ins["L0_attn_W_proj"].astype(np.float32),
            "bpr": vec_chunks(ins["L0_attn_b_proj"]),
        }
    else:
        bqkv = np.concatenate([vec_chunks(ins["L2_attn_bq"]),
                               vec_chunks(ins["L2_attn_bk"]),
                               vec_chunks(ins["L2_attn_bv"])], axis=1)
        extras = {
            "wq": ins["L2_attn_Wq"].astype(np.float32),
            "wk": ins["L2_attn_Wk"].astype(np.float32),
            "wv": ins["L2_attn_Wv"].astype(np.float32),
            "bqkv": bqkv,
            "jrow": np.tile(np.arange(SEQ, dtype=np.float32), (128, 1)),
            "wpr": ins["L2_attn_Wo"].astype(np.float32),
            "bpr": vec_chunks(ins["L2_attn_bo"]),
        }
    out = np.empty((NB, DM, SEQ), np.float32)
    for half in range(2):
        t0 = half * TOW
        nc = build_causal_mha(t0, kind)
        in_maps = []
        cores = []
        for b in range(NB):
            cores.append(2 * b + half)
            im = {"xf": np.ascontiguousarray(x_fm[b]),
                  "masks": masks4, "ident_in": ident}
            im.update(extras)
            in_maps.append(im)
        res = run_bass_kernel_spmd(nc, in_maps, core_ids=cores)
        for i, b in enumerate(range(NB)):
            out[b, :, t0:t0 + TOW] = res.results[i]["xo"]
    return out


def kernel(**inputs):
    x_fm = to_fm(np.asarray(inputs["L0_attn_x"], np.float32))
    x_fm = run_causal_mha(x_fm, "l0", inputs)
    x_fm = run_l0_ffn(x_fm, inputs)
    x_fm = run_l1_attn(x_fm, inputs)
    x_fm = run_l1_ffn(x_fm, inputs)
    x_fm = run_causal_mha(x_fm, "l2", inputs)
    x_fm = run_l2_ffn(x_fm, inputs)
    x_fm = run_l3_attn(x_fm, inputs)
    x_fm = run_l3_ffn(x_fm, inputs)
    return from_fm(x_fm)
